# version 14
# baseline (speedup 1.0000x reference)
import numpy as np
import concourse.bass as bass
import concourse.mybir as mybir
from concourse.bass_utils import run_bass_kernel_spmd
from contextlib import ExitStack

F32 = mybir.dt.float32
F16 = mybir.dt.float16
BF16 = mybir.dt.bfloat16
AF = mybir.ActivationFunctionType
ALU = mybir.AluOpType
AX = mybir.AxisListType

N_CORES = 8
S = 2048
T = 2 * S
D = 1024
HID = 4096
NKT = 16
NQC = 4
FC = 1024
NFC = T // FC
EXP_SHIFT = -32.0
EPS = 1e-5
DW = 2432

ENGS = ("te", "vx", "sc", "gp", "sy")


class Op:
    __slots__ = ("eng", "fn", "sem", "amt", "gidx", "cum", "need_inc", "deps",
                 "dma")


class Prog:

    def __init__(self, nc):
        self.nc = nc
        self.ops = {e: [] for e in ENGS}
        self.all = []
        self.dma_sem_owner = {}

    def add(self, eng, fn, deps=(), sem=None, dma=False):
        op = Op()
        op.eng = eng
        op.fn = fn
        op.dma = dma
        op.sem = sem if sem is not None else eng
        if dma:
            assert sem is not None
            assert self.dma_sem_owner.setdefault(sem, eng) == eng
        op.amt = 16 if dma else 1
        op.deps = [d for d in deps if d is not None]
        op.gidx = len(self.all)
        op.need_inc = dma
        op.cum = 0
        for d in op.deps:
            assert d.gidx < op.gidx
        self.ops[eng].append(op)
        self.all.append(op)
        return op

    def fence(self):
        return [self.ops[e][-1] for e in ENGS if self.ops[e]]

    def finalize_and_emit(self):
        for op in self.all:
            for d in op.deps:
                if d.dma or d.eng != op.eng:
                    d.need_inc = True
        counters = {}
        for e in ENGS:
            for op in self.ops[e]:
                if op.need_inc:
                    counters[op.sem] = counters.get(op.sem, 0) + op.amt
                    op.cum = counters[op.sem]
        nc = self.nc
        handles = {n: nc.alloc_semaphore(name="sm_" + n) for n in counters}
        prog = self

        def emitter(eng_name):
            def f(eng):
                last_wait = {}
                for op in prog.ops[eng_name]:
                    waits = {}
                    for d in op.deps:
                        if (d.eng == op.eng and not d.dma) or not d.need_inc:
                            continue
                        waits[d.sem] = max(waits.get(d.sem, 0), d.cum)
                    for sn in sorted(waits):
                        v = waits[sn]
                        if last_wait.get(sn, -1) < v:
                            eng.wait_ge(handles[sn], v)
                            last_wait[sn] = v
                    inst = op.fn(eng)
                    if op.need_inc:
                        inst.then_inc(handles[op.sem], op.amt)
            return f

        with nc.Block() as block:
            block.tensor(emitter("te"))
            block.vector(emitter("vx"))
            block.scalar(emitter("sc"))
            block.gpsimd(emitter("gp"))
            block.sync(emitter("sy"))


class Buf:

    def __init__(self):
        self.writers = []
        self.readers = []

    def rdeps(self):
        return list(self.writers)

    def wdeps(self):
        return self.readers + self.writers

    def wrote(self, op):
        self.writers = [op]
        self.readers = []

    def also_wrote(self, op):
        self.writers.append(op)

    def read(self, op):
        self.readers.append(op)


def build_program():
    nc = bass.Bass("TRN2", target_bir_lowering=False)
    P = Prog(nc)

    x_hbm = nc.dram_tensor("x", [T, D], F32, kind="ExternalInput").ap()
    wqt = nc.dram_tensor("wqt", [128, 8 * 1024], F32, kind="ExternalInput").ap()
    wkt = nc.dram_tensor("wkt", [128, 8 * 1024], F32, kind="ExternalInput").ap()
    wvt = nc.dram_tensor("wvt", [128, 8 * 1024], F32, kind="ExternalInput").ap()
    wot = nc.dram_tensor("wot", [128, 8 * 1024], F32, kind="ExternalInput").ap()
    bo_in = nc.dram_tensor("bo_bc", [128, D], F32, kind="ExternalInput").ap()
    w1t = nc.dram_tensor("w1t", [32, 128, 8 * 128], F32, kind="ExternalInput").ap()
    w2t = nc.dram_tensor("w2t", [32, 128, 8 * 128], F32, kind="ExternalInput").ap()
    h1t = nc.dram_tensor("h1t", [32, 128, 32 * 128], F32, kind="ExternalInput").ap()
    h2t = nc.dram_tensor("h2t", [32, 128, 32 * 128], F32, kind="ExternalInput").ap()
    w3t = nc.dram_tensor("w3t", [8, 128, 32 * 128], F32, kind="ExternalInput").ap()
    s1_in = nc.dram_tensor("s1r", [128, 32], F32, kind="ExternalInput").ap()
    s2_in = nc.dram_tensor("s2r", [128, 32], F32, kind="ExternalInput").ap()
    sh1_in = nc.dram_tensor("sh1r", [128, 32], F32, kind="ExternalInput").ap()
    sh2_in = nc.dram_tensor("sh2r", [128, 32], F32, kind="ExternalInput").ap()
    s3_in = nc.dram_tensor("s3r", [128, 8], F32, kind="ExternalInput").ap()
    dist_in = nc.dram_tensor("dist", [128, 2 * 3968], F16, kind="ExternalInput").ap()
    idf_in = nc.dram_tensor("identf", [128, 128], F32, kind="ExternalInput").ap()
    sel_in = nc.dram_tensor("selrow", [128, 256], F32, kind="ExternalInput").ap()
    si_in = nc.dram_tensor("slopei", [128, 8 * 128], F16, kind="ExternalInput").ap()
    out_hbm = nc.dram_tensor("out", [T, D], F32, kind="ExternalOutput").ap()
    xn_dram = nc.dram_tensor("xn_scr", [8, 128, 8 * 512], BF16, kind="Internal").ap()
    attn_dram = nc.dram_tensor("attn_scr", [T, D], F32, kind="Internal").ap()

    esg = ExitStack()

    def sbg(name, shape, dtype):
        return esg.enter_context(nc.sbuf_tensor(name, shape, dtype))

    identf = sbg("idf_sb", [128, 128], F32)
    selrow = sbg("sel_sb", [128, 256], BF16)
    s1_sb = sbg("s1_sb", [128, 32], F32)
    s2_sb = sbg("s2_sb", [128, 32], F32)
    sh1_sb = sbg("sh1_sb", [128, 32], F32)
    sh2_sb = sbg("sh2_sb", [128, 32], F32)
    s3_sb = sbg("s3_sb", [128, 8], F32)
    biasC = sbg("biasC", [128, 1], F32)

    pp = [esg.enter_context(nc.psum_tensor("pp%d" % i, [128, 1024], F32))
          for i in range(4)]
    ps = [pp[i // 2][:, (i % 2) * 512:(i % 2) * 512 + 512] for i in range(8)]
    pse_pv = ps[6]
    pso_pv = ps[7]
    ps6_lo = pp[0][0:64, 0:512]
    ps7_hi = pp[0][64:128, 512:1024]
    psb = [Buf() for _ in range(8)]

    ld_idf = P.add("gp", lambda e: e.dma_start(identf[:, :], idf_in),
                   sem="dC", dma=True)
    ld_sel = P.add("gp", lambda e: e.dma_start(selrow[:, :], sel_in),
                   sem="dC", dma=True)
    scl = {}
    for nm, src, dst in (("s1", s1_in, s1_sb), ("s2", s2_in, s2_sb),
                         ("sh1", sh1_in, sh1_sb), ("sh2", sh2_in, sh2_sb),
                         ("s3", s3_in, s3_sb)):
        scl[nm] = P.add("sy", lambda e, d=dst, s=src: e.dma_start(d[:, :], s),
                        sem="dCs", dma=True)
    op_biasC = P.add("gp", lambda e: e.memset(biasC[:, :], EXP_SHIFT))

    xn_grp_writers = [[] for _ in range(8)]
    attn_row_writers = [[] for _ in range(32)]

    es1 = ExitStack()

    def sb1(name, shape, dtype):
        return es1.enter_context(nc.sbuf_tensor(name, shape, dtype))

    x_tile = sb1("x_tile", [128, 3 * D], F32)
    x_tile_b = [Buf() for _ in range(3)]
    xn_tm = sb1("xn_tm", [128, 2 * D], F32)
    xn_tm_b = [Buf(), Buf()]
    sq_scr = sb1("sq_scr", [128, 2 * D], BF16)
    sq_b = [Buf(), Buf()]
    stat = sb1("stat", [128, 24], F32)
    stat_b = [Buf() for _ in range(4)]
    xn_stage = sb1("xn_stage", [128, 2 * 4096], BF16)
    xn_stage_b = [[Buf() for _ in range(8)] for _ in range(2)]

    for tt in range(32):
        sl3, sl2, sl4 = tt % 3, tt % 2, tt % 4
        g = tt // 4
        gs = g % 2
        xa = x_tile[:, sl3 * D:(sl3 + 1) * D]
        xb = x_tile_b[sl3]
        ldx = P.add("sy", lambda e, d=xa, o=tt: e.dma_start(
            d, x_hbm[o * 128:(o + 1) * 128, :]),
            deps=xb.wdeps(), sem="dT%d" % sl3, dma=True)
        xb.wrote(ldx)
        st = stat[:, sl4 * 6:(sl4 + 1) * 6]
        sbu = stat_b[sl4]
        o_sum = P.add("vx", lambda e, s=xa, c=st: e.reduce_sum(
            c[:, 0:1], s, axis=AX.X, opt_input=False, opt_output=False),
            deps=xb.rdeps() + sbu.wdeps())
        xb.read(o_sum)
        sbu.wrote(o_sum)
        o_sq = P.add("sc", lambda e, s=xa, c=st,
                     q=sq_scr[:, sl2 * D:(sl2 + 1) * D]: e.activation(
                         q, s, AF.Square, accum_out=c[:, 1:2]),
                     deps=xb.rdeps() + [o_sum] + sq_b[sl2].wdeps())
        xb.read(o_sq)
        sq_b[sl2].wrote(o_sq)
        sbu.also_wrote(o_sq)
        o_mu = P.add("vx", lambda e, c=st: e.tensor_scalar_mul(
            c[:, 2:3], c[:, 0:1], 1.0 / D), deps=[o_sum, o_sq])
        o_ex2 = P.add("vx", lambda e, c=st: e.tensor_scalar_mul(
            c[:, 3:4], c[:, 1:2], 1.0 / D), deps=[o_sq])
        o_nmu2 = P.add("vx", lambda e, c=st: e.scalar_tensor_tensor(
            c[:, 4:5], c[:, 2:3], -1.0, c[:, 2:3], op0=ALU.mult, op1=ALU.mult),
            deps=[o_mu])
        o_var = P.add("vx", lambda e, c=st: e.tensor_add(
            c[:, 3:4], c[:, 3:4], c[:, 4:5]), deps=[o_ex2, o_nmu2])
        o_veps = P.add("vx", lambda e, c=st: e.tensor_scalar_add(
            c[:, 3:4], c[:, 3:4], EPS), deps=[o_var])
        o_std = P.add("sc", lambda e, c=st: e.activation(
            c[:, 4:5], c[:, 3:4], AF.Sqrt), deps=[o_veps])
        o_rstd = P.add("vx", lambda e, c=st: e.reciprocal(
            c[:, 5:6], c[:, 4:5]), deps=[o_std])
        for o in (o_mu, o_ex2, o_nmu2, o_var, o_veps, o_std, o_rstd):
            sbu.also_wrote(o)
        o_gate = P.add("sc", lambda e: e.nop(), deps=[o_rstd, o_mu])
        xt = xn_tm[:, sl2 * D:(sl2 + 1) * D]
        o_norm = P.add("vx", lambda e, s=xa, d=xt, c=st: e.tensor_scalar(
            d, s, c[:, 2:3], c[:, 5:6], op0=ALU.subtract, op1=ALU.mult),
            deps=xb.rdeps() + [o_gate] + xn_tm_b[sl2].wdeps())
        xb.read(o_norm)
        sbu.read(o_norm)
        xn_tm_b[sl2].wrote(o_norm)
        for dch in range(8):
            bank = 6 + (dch % 2)
            o_tr = P.add("te", lambda e, s=xt[:, dch * 128:(dch + 1) * 128],
                         bb=bank: e.transpose(ps[bb][:, 0:128], s, identf[:, :]),
                         deps=[ld_idf, o_norm] + psb[bank].wdeps())
            xn_tm_b[sl2].read(o_tr)
            psb[bank].wrote(o_tr)
            stb = xn_stage_b[gs][dch]
            dst = xn_stage[:, gs * 4096 + dch * 512 + sl4 * 128:
                           gs * 4096 + dch * 512 + (sl4 + 1) * 128]
            o_ev = P.add("vx", lambda e, d=dst, bb=bank: e.tensor_copy(
                d, ps[bb][:, 0:128]),
                deps=[o_tr] + (stb.wdeps() if sl4 == 0 else []))
            psb[bank].read(o_ev)
            if sl4 == 0:
                stb.wrote(o_ev)
            else:
                stb.also_wrote(o_ev)
        if sl4 == 3:
            for dch in range(8):
                stb = xn_stage_b[gs][dch]
                src = xn_stage[:, gs * 4096 + dch * 512:gs * 4096 + (dch + 1) * 512]
                o_st = P.add("sy", lambda e, s=src, dd=dch, gg=g: e.dma_start(
                    xn_dram[gg][:, dd * 512:(dd + 1) * 512], s),
                    deps=stb.wdeps(), sem="dXS", dma=True)
                stb.read(o_st)
                xn_grp_writers[g].append(o_st)

    ln_fence = P.fence()
    es1.close()

    es2 = ExitStack()

    def sb2(name, shape, dtype):
        return es2.enter_context(nc.sbuf_tensor(name, shape, dtype))

    wq_sb = sb2("wq_sb", [128, 8 * 1024], BF16)
    wo_sb = sb2("wo_sb", [128, 8 * 1024], BF16)
    wk_sb = sb2("wk_sb", [128, 8 * 1024], BF16)
    wv_sb = sb2("wv_sb", [128, 8 * 1024], BF16)
    dist_w = sb2("dist_w", [128, 2 * DW], F16)
    si_sb = sb2("si_sb", [128, 8 * 128], F16)
    bo_sb = sb2("bo_sb", [128, D], F32)
    K_dup = sb2("K_dup", [128, 8 * 2048], BF16)
    V_sb = sb2("V_sb", [128, 16 * 8 * 192], BF16)
    xn_at = sb2("xn_at", [128, 2 * 8 * 512], BF16)
    Q_sb = sb2("Q_sb", [128, 8 * 512], BF16)
    P_sb = sb2("P_sb", [128, 6 * 512], BF16)
    PV_sb = sb2("PV_sb", [128, 8 * 512], BF16)
    sums_sb = sb2("sums_sb", [128, 2 * 512], BF16)
    bc_sb = sb2("bc_sb", [128, 512], F32)
    ast = sb2("ast", [128, D], F32)

    K_b = [[Buf() for _ in range(NKT)] for _ in range(8)]
    V_b = [[Buf() for _ in range(NKT)] for _ in range(8)]
    xn_at_b = [Buf(), Buf()]
    Q_b = [Buf() for _ in range(8)]
    P_b = [Buf() for _ in range(3)]
    PV_b = [Buf() for _ in range(8)]
    sums_b = [Buf() for _ in range(8)]
    bc_b = Buf()
    ast_b = Buf()
    dist_b = Buf()

    def wload(dst_t, src):
        return P.add("gp", lambda e, d=dst_t, s=src: e.dma_start(d[:, :], s),
                     deps=ln_fence, sem="dC", dma=True)

    ld_wq = wload(wq_sb, wqt)
    ld_wo = wload(wo_sb, wot)
    ld_wk = wload(wk_sb, wkt)
    ld_wv = wload(wv_sb, wvt)
    ld_bo = P.add("sy", lambda e: e.dma_start(bo_sb[:, :], bo_in),
                  deps=ln_fence, sem="dCs", dma=True)
    ld_si = P.add("sy", lambda e: e.dma_start(si_sb[:, :], si_in),
                  deps=ln_fence, sem="dCs", dma=True)

    op_vz = P.add("gp", lambda e: e.memset(V_sb[:, :], 0.0), deps=ln_fence)
    ones_ap = bass.AP(V_sb[:, :].tensor, 64,
                      [[16 * 8 * 192, 128], [192, 16 * 8], [1, 1]])
    op_vo = P.add("gp", lambda e: e.memset(ones_ap, 1.0), deps=[op_vz])
    v_init = [op_vo]

    def xn_slot_load(slot, scol, extra):
        dst = xn_at[:, slot * 4096:(slot + 1) * 4096]
        g = scol // 512
        op = P.add("sy", lambda e, d=dst, s=xn_dram[g]: e.dma_start(d, s),
                   deps=xn_at_b[slot].wdeps() + xn_grp_writers[g] + extra,
                   sem="dX%d" % slot, dma=True)
        xn_at_b[slot].wrote(op)
        return op

    first_kv = True
    for s in range(2):
        for j in range(4):
            slot = j % 2
            o_x = xn_slot_load(slot, s * S + j * 512, ln_fence if first_kv else [])
            first_kv = False
            xslot = xn_at[:, slot * 4096:(slot + 1) * 4096]
            for c in range(8):
                bank = c % 4
                mms = []
                for i in range(8):
                    dep = (psb[bank].wdeps() + [ld_wk, o_x]) if i == 0 else []
                    mm = P.add("te", lambda e, ii=i, cc=c, bb=bank, xs=xslot:
                               e.matmul(ps[bb],
                                        wk_sb[:, ii * 1024 + cc * 128:
                                              ii * 1024 + (cc + 1) * 128],
                                        xs[:, ii * 512:(ii + 1) * 512],
                                        start=(ii == 0), stop=(ii == 7)),
                               deps=dep)
                    mms.append(mm)
                xn_at_b[slot].read(mms[-1])
                psb[bank].wrote(mms[-1])
                kt0 = j * 4
                kdeps = []
                for kk in range(4):
                    kdeps += K_b[c][kt0 + kk].wdeps()
                dst = K_dup[:, c * 2048 + j * 512:c * 2048 + (j + 1) * 512]
                o_ev = P.add("vx", lambda e, d=dst, bb=bank: e.tensor_copy(
                    d, ps[bb]), deps=[mms[-1]] + kdeps)
                psb[bank].read(o_ev)
                for kk in range(4):
                    K_b[c][kt0 + kk].wrote(o_ev)
            for tt4 in range(4):
                kt = j * 4 + tt4
                for half in range(2):
                    bank = (tt4 * 2 + half) % 4
                    mms = []
                    for i in range(8):
                        dep = (psb[bank].wdeps() + [ld_wv, o_x] + v_init) \
                            if i == 0 else []
                        mm = P.add("te", lambda e, ii=i, hh=half, t4=tt4,
                                   bb=bank, xs=xslot:
                                   e.matmul(ps[bb],
                                            xs[:, ii * 512 + t4 * 128:
                                               ii * 512 + (t4 + 1) * 128],
                                            wv_sb[:, ii * 1024 + hh * 512:
                                                  ii * 1024 + (hh + 1) * 512],
                                            start=(ii == 0), stop=(ii == 7)),
                                   deps=dep)
                        mms.append(mm)
                    xn_at_b[slot].read(mms[-1])
                    psb[bank].wrote(mms[-1])
                    for h2 in range(4):
                        kvh = half * 4 + h2
                        blk = (kt * 8 + kvh) * 192
                        vb = V_b[kvh][kt]
                        dst = bass.AP(V_sb[:, :].tensor, blk,
                                      [[16 * 8 * 192, 128], [128, 2], [1, 64]])
                        o_ev = P.add("vx", lambda e, d=dst, bb=bank, hx=h2:
                                     e.tensor_copy(
                                         d, ps[bb][:, hx * 128:(hx + 1) * 128]),
                                     deps=[mms[-1]] + vb.wdeps() + v_init)
                        psb[bank].read(o_ev)
                        vb.wrote(o_ev)

        for qc in range(NQC):
            xslot = (s * NQC + qc) % 2
            o_x = xn_slot_load(xslot, s * S + qc * 512, [])
            xs = xn_at[:, xslot * 4096:(xslot + 1) * 4096]
            dsrc = bass.AP(dist_in.tensor, qc * 512,
                           [[2 * 3968, 128], [3968, 2], [1, DW]])
            o_dist = P.add("sy", lambda e, s2=dsrc: e.dma_start(
                dist_w[:, :].rearrange("p (s c) -> p s c", s=2), s2),
                deps=dist_b.wdeps() + (ln_fence if s == 0 and qc == 0 else []),
                sem="dDW", dma=True)
            dist_b.wrote(o_dist)
            for hp in range(8):
                bank = hp % 2
                mms = []
                for i in range(8):
                    dep = (psb[bank].wdeps() + [ld_wq, o_x]) if i == 0 else []
                    mm = P.add("te", lambda e, ii=i, h=hp, bb=bank, x2=xs:
                               e.matmul(ps[bb],
                                        wq_sb[:, ii * 1024 + h * 128:
                                              ii * 1024 + (h + 1) * 128],
                                        x2[:, ii * 512:(ii + 1) * 512],
                                        start=(ii == 0), stop=(ii == 7)),
                               deps=dep)
                    mms.append(mm)
                xn_at_b[xslot].read(mms[-1])
                psb[bank].wrote(mms[-1])
                o_ev = P.add("vx", lambda e, h=hp, bb=bank: e.tensor_copy(
                    Q_sb[:, h * 512:(h + 1) * 512], ps[bb]),
                    deps=[mms[-1]] + Q_b[hp].wdeps())
                psb[bank].read(o_ev)
                Q_b[hp].wrote(o_ev)

            for hp in range(8):
                kvh = hp
                slope_e = float(2.0 ** (-(2 * hp + 1)))
                slope_o = float(2.0 ** (-(2 * hp + 2)))
                pv_e_mm = pv_o_mm = None
                exps = {}

                def emit_scores(kt):
                    bp = kt % 3
                    bank_e, bank_o = bp * 2, bp * 2 + 1
                    uloc = 1920 - 128 * kt
                    kslc = K_dup[:, kvh * 2048 + kt * 128:
                                 kvh * 2048 + (kt + 1) * 128]
                    mm_se = P.add("te", lambda e, ks=kslc, h=hp, bb=bank_e:
                                  e.matmul(ps[bb], ks[0:64, :],
                                           Q_sb[0:64, h * 512:(h + 1) * 512],
                                           start=True, stop=False,
                                           tile_position=(0, 0)),
                                  deps=K_b[kvh][kt].rdeps() + Q_b[hp].rdeps()
                                  + psb[bank_e].wdeps())
                    K_b[kvh][kt].read(mm_se)
                    Q_b[hp].read(mm_se)
                    psb[bank_e].wrote(mm_se)
                    mm_so = P.add("te", lambda e, ks=kslc, h=hp, bb=bank_o:
                                  e.matmul(ps[bb], ks[64:128, :],
                                           Q_sb[64:128, h * 512:(h + 1) * 512],
                                           start=True, stop=False,
                                           tile_position=(64, 0)),
                                  deps=K_b[kvh][kt].rdeps() + Q_b[hp].rdeps()
                                  + psb[bank_o].wdeps())
                    K_b[kvh][kt].read(mm_so)
                    Q_b[hp].read(mm_so)
                    psb[bank_o].wrote(mm_so)
                    mm_be = P.add("te", lambda e, h=hp, u=uloc, bb=bank_e:
                                  e.matmul(ps[bb], si_sb[:, h * 128:(h + 1) * 128],
                                           dist_w[:, u:u + 512],
                                           start=False, stop=True),
                                  deps=[o_dist, ld_si])
                    psb[bank_e].also_wrote(mm_be)
                    dist_b.read(mm_be)
                    mm_bo = P.add("te", lambda e, h=hp, u=uloc, bb=bank_o:
                                  e.matmul(ps[bb], si_sb[:, h * 128:(h + 1) * 128],
                                           dist_w[:, DW + u:DW + u + 512],
                                           start=False, stop=True),
                                  deps=[o_dist, ld_si])
                    psb[bank_o].also_wrote(mm_bo)
                    dist_b.read(mm_bo)
                    ppair = P_sb[:, bp * 1024:(bp + 1) * 1024]
                    expp = P.add("sc", lambda e, d=ppair, b2=bp: e.activation(
                        d, pp[b2][:, :], AF.Exp, bias=biasC[:, :], scale=0.125),
                        deps=[mm_be, mm_bo, op_biasC] + P_b[bp].wdeps())
                    psb[bank_e].read(expp)
                    psb[bank_o].read(expp)
                    P_b[bp].wrote(expp)
                    exps[kt] = (expp, P_sb[:, bp * 1024:bp * 1024 + 512],
                                P_sb[:, bp * 1024 + 512:(bp + 1) * 1024])

                def emit_pv(kt):
                    nonlocal pv_e_mm, pv_o_mm
                    bp = kt % 3
                    expp, pe, po = exps.pop(kt)
                    exp_e = exp_o = expp
                    blk = (kt * 8 + kvh) * 192
                    depe = (psb[6].wdeps() if kt == 0 else []) + [exp_e] \
                        + V_b[kvh][kt].rdeps()
                    pv_e_mm = P.add("te", lambda e, b2=blk, p2=pe, k2=kt:
                                    e.matmul(pse_pv,
                                             V_sb[:, b2:b2 + 128], p2,
                                             start=(k2 == 0), stop=(k2 == NKT - 1)),
                                    deps=depe)
                    V_b[kvh][kt].read(pv_e_mm)
                    P_b[bp].read(pv_e_mm)
                    psb[6].wrote(pv_e_mm) if kt == 0 else psb[6].also_wrote(pv_e_mm)
                    depo = (psb[7].wdeps() if kt == 0 else []) + [exp_o] \
                        + V_b[kvh][kt].rdeps()
                    pv_o_mm = P.add("te", lambda e, b2=blk, p2=po, k2=kt:
                                    e.matmul(pso_pv,
                                             V_sb[:, b2 + 64:b2 + 192], p2,
                                             start=(k2 == 0), stop=(k2 == NKT - 1)),
                                    deps=depo)
                    V_b[kvh][kt].read(pv_o_mm)
                    P_b[bp].read(pv_o_mm)
                    psb[7].wrote(pv_o_mm) if kt == 0 else psb[7].also_wrote(pv_o_mm)

                for kt in range(NKT):
                    emit_scores(kt)
                    if kt >= 2:
                        emit_pv(kt - 2)
                emit_pv(NKT - 2)
                emit_pv(NKT - 1)
                pvb = PV_b[hp]
                smb = sums_b[hp % 2]
                c1 = P.add("vx", lambda e, h=hp: e.tensor_copy(
                    PV_sb[0:64, h * 512:(h + 1) * 512], pse_pv[0:64, :]),
                    deps=[pv_e_mm] + pvb.wdeps())
                psb[6].read(c1)
                c2 = P.add("vx", lambda e, h=hp: e.tensor_copy(
                    sums_sb[64:65, (h % 2) * 512:(h % 2 + 1) * 512], pse_pv[64:65, :]),
                    deps=[pv_e_mm] + smb.wdeps())
                psb[6].read(c2)
                c3 = P.add("vx", lambda e, h=hp: e.tensor_copy(
                    PV_sb[64:128, h * 512:(h + 1) * 512], pso_pv[64:128, :]),
                    deps=[pv_o_mm] + pvb.wdeps())
                psb[7].read(c3)
                c4 = P.add("vx", lambda e, h=hp: e.tensor_copy(
                    sums_sb[0:1, (h % 2) * 512:(h % 2 + 1) * 512], pso_pv[0:1, :]),
                    deps=[pv_o_mm] + smb.wdeps())
                psb[7].read(c4)
                pvb.wrote(c1)
                pvb.also_wrote(c3)
                smb.wrote(c2)
                smb.also_wrote(c4)
                m1 = P.add("te", lambda e, h=hp: e.matmul(
                    ps[0], selrow[64:65, 0:128],
                    sums_sb[64:65, (h % 2) * 512:(h % 2 + 1) * 512],
                    start=True, stop=True),
                    deps=[c2, ld_sel] + psb[0].wdeps())
                psb[0].wrote(m1)
                smb.read(m1)
                m2 = P.add("te", lambda e, h=hp: e.matmul(
                    ps[1], selrow[0:1, 128:256],
                    sums_sb[0:1, (h % 2) * 512:(h % 2 + 1) * 512],
                    start=True, stop=True),
                    deps=[c4, ld_sel] + psb[1].wdeps())
                psb[1].wrote(m2)
                smb.read(m2)
                r1 = P.add("vx", lambda e: e.reciprocal(
                    bc_sb[0:64, :], ps6_lo), deps=[m1] + bc_b.wdeps())
                psb[0].read(r1)
                r2 = P.add("vx", lambda e: e.reciprocal(
                    bc_sb[64:128, :], ps7_hi), deps=[m2] + bc_b.wdeps())
                psb[1].read(r2)
                bc_b.wrote(r1)
                bc_b.also_wrote(r2)
                n1 = P.add("vx", lambda e, h=hp: e.tensor_mul(
                    PV_sb[0:64, h * 512:(h + 1) * 512],
                    PV_sb[0:64, h * 512:(h + 1) * 512], bc_sb[0:64, :]),
                    deps=[r1, c1])
                bc_b.read(n1)
                pvb.also_wrote(n1)
                n2 = P.add("vx", lambda e, h=hp: e.tensor_mul(
                    PV_sb[64:128, h * 512:(h + 1) * 512],
                    PV_sb[64:128, h * 512:(h + 1) * 512], bc_sb[64:128, :]),
                    deps=[r2, c3])
                bc_b.read(n2)
                pvb.also_wrote(n2)

            pv_all_writers = [w for b in PV_b for w in b.writers]
            for j in range(4):
                bank = j % 2
                tok0 = s * S + qc * 512 + j * 128
                evs = []
                for half in range(2):
                    mms = []
                    for h in range(8):
                        dep = []
                        if h == 0:
                            dep = psb[bank].wdeps() + [ld_wo] + pv_all_writers
                        mm = P.add("te", lambda e, h2=h, j2=j, hf=half, bb=bank:
                                   e.matmul(ps[bb],
                                            PV_sb[:, h2 * 512 + j2 * 128:
                                                  h2 * 512 + (j2 + 1) * 128],
                                            wo_sb[:, h2 * 1024 + hf * 512:
                                                  h2 * 1024 + (hf + 1) * 512],
                                            start=(h2 == 0), stop=(h2 == 7)),
                                   deps=dep)
                        mms.append(mm)
                    for b in PV_b:
                        b.read(mms[-1])
                    psb[bank].wrote(mms[-1])
                    dst = ast[:, half * 512:(half + 1) * 512]
                    o_ev = P.add("vx", lambda e, d=dst, hf=half, bb=bank:
                                 e.tensor_add(d, ps[bb],
                                              bo_sb[:, hf * 512:(hf + 1) * 512]),
                                 deps=[mms[-1], ld_bo]
                                 + (ast_b.wdeps() if half == 0 else []))
                    psb[bank].read(o_ev)
                    if half == 0:
                        ast_b.wrote(o_ev)
                    else:
                        ast_b.also_wrote(o_ev)
                    evs.append(o_ev)
                o_st = P.add("sy", lambda e, t0=tok0: e.dma_start(
                    attn_dram[t0:t0 + 128, :], ast[:, :]),
                    deps=evs, sem="dA", dma=True)
                ast_b.read(o_st)
                attn_row_writers[tok0 // 128].append(o_st)

    attn_fence = P.fence()
    es2.close()

    es3 = ExitStack()

    def sb3(name, shape, dtype):
        return es3.enter_context(nc.sbuf_tensor(name, shape, dtype))

    H_sb = sb3("H_sb", [128, 32 * 1024], BF16)
    G_sb = sb3("G_sb", [128, 32 * 1024], BF16)
    wst = sb3("wst", [128, 2 * 4096], BF16)
    xn_fc = sb3("xn_fc", [128, 8 * 1024], BF16)
    ffnE = sb3("ffnE", [128, 2 * 512], F32)
    ost = sb3("ost", [128, 4 * 1024], F32)
    atr = sb3("atr", [128, 2 * 1024], F32)
    xrl = sb3("xrl", [128, 2 * 1024], F32)

    H_b = [Buf() for _ in range(32)]
    G_b = [Buf() for _ in range(32)]
    wst_b = [Buf(), Buf()]
    xn_fc_b = Buf()
    ffnE_b = [Buf(), Buf()]
    ost_b = [Buf() for _ in range(4)]
    atr_b = [Buf(), Buf()]
    xrl_b = [Buf(), Buf()]

    wst_ctr = [0]

    def next_wslot():
        s = wst_ctr[0] % 2
        wst_ctr[0] += 1
        return s

    def wstream(slot, src_slice, extra, nrows):
        dst = wst[:, slot * 4096:slot * 4096 + nrows * 128].rearrange(
            "p (i o) -> p i o", i=nrows)
        op = P.add("gp", lambda e, d=dst, s=src_slice: e.dma_start(d, s),
                   deps=wst_b[slot].wdeps() + extra, sem="dW%d" % slot, dma=True)
        wst_b[slot].wrote(op)
        return op

    def emit_xn_load(fc, fence0):
        ops = []
        for g2 in range(2):
            g = 2 * fc + g2
            op = P.add("sc", lambda e, gg=g, g3=g2: e.dma_start(
                xn_fc[:, g3 * 4096:(g3 + 1) * 4096], xn_dram[gg]),
                deps=(xn_fc_b.wdeps() if g2 == 0 else []) + xn_grp_writers[g]
                + fence0, sem="dXF", dma=True)
            ops.append(op)
        xn_fc_b.wrote(ops[0])
        xn_fc_b.also_wrote(ops[1])
        return ops

    for fc in range(NFC):
        if fc == 0:
            o_xns = emit_xn_load(0, attn_fence)
        else:
            o_xns = nxt
        for which in ("w1", "w2"):
            wt = w1t if which == "w1" else w2t
            for o in range(32):
                slot = next_wslot()
                wsrc = wt[o]
                extra = attn_fence if (fc == 0 and which == "w1" and o < 2) else []
                o_w = wstream(slot, wsrc, extra, 8)
                bankp = (o % 2) * 2
                mm_last = [None, None]
                for i in range(8):
                    for tc2 in range(2):
                        bank = bankp + tc2
                        dep = (psb[bank].wdeps() + [o_w] + o_xns) if i == 0 else []
                        mm = P.add("te", lambda e, ii=i, t2=tc2, bb=bankp, sl=slot:
                                   e.matmul(ps[bb + t2],
                                            wst[:, sl * 4096 + ii * 128:
                                                sl * 4096 + (ii + 1) * 128],
                                            xn_fc[:, t2 * 4096 + ii * 512:
                                                  t2 * 4096 + ii * 512 + 512],
                                            start=(ii == 0), stop=(ii == 7)),
                                   deps=dep)
                        if i == 0:
                            psb[bank].wrote(mm)
                        elif i == 7:
                            psb[bank].also_wrote(mm)
                        mm_last[tc2] = mm
                for tc2 in range(2):
                    wst_b[slot].read(mm_last[tc2])
                    xn_fc_b.read(mm_last[tc2])
                for tc2 in range(2):
                    bank = bankp + tc2
                    hcol = o * 1024 + tc2 * 512
                    if which == "w1":
                        o_ev = P.add("sc", lambda e, hc=hcol, oo=o, bb=bank:
                                     e.activation(H_sb[:, hc:hc + 512],
                                                  ps[bb], AF.Silu,
                                                  scale=s1_sb[:, oo:oo + 1]),
                                     deps=[mm_last[tc2], scl["s1"]]
                                     + H_b[o].wdeps())
                    else:
                        o_ev = P.add("vx", lambda e, hc=hcol, oo=o, bb=bank:
                                     e.scalar_tensor_tensor(
                                         H_sb[:, hc:hc + 512], ps[bb],
                                         s2_sb[:, oo:oo + 1],
                                         H_sb[:, hc:hc + 512],
                                         op0=ALU.mult, op1=ALU.mult),
                                     deps=[mm_last[tc2], scl["s2"]]
                                     + (H_b[o].rdeps() if which == "w2" else []))
                    psb[bank].read(o_ev)
                    if which == "w1" and tc2 == 0:
                        H_b[o].wrote(o_ev)
                    else:
                        H_b[o].also_wrote(o_ev)

        if fc + 1 < NFC:
            nxt = emit_xn_load(fc + 1, [])
        for which in ("h1", "h2"):
            ht = h1t if which == "h1" else h2t
            ssb = sh1_sb if which == "h1" else sh2_sb
            sscl = scl["sh1"] if which == "h1" else scl["sh2"]
            SRC_b, DST_b = (H_b, G_b) if which == "h1" else (G_b, H_b)
            src_t, dst_t = (H_sb, G_sb) if which == "h1" else (G_sb, H_sb)
            src_writers = [w for b in SRC_b for w in b.writers]
            for o in range(32):
                slot = next_wslot()
                wsrc = ht[o]
                o_w = wstream(slot, wsrc, [], 32)
                bankp = (o % 2) * 2
                mm_last = [None, None]
                for i in range(32):
                    for tc2 in range(2):
                        bank = bankp + tc2
                        dep = []
                        if i == 0:
                            dep = psb[bank].wdeps() + [o_w] + src_writers
                        mm = P.add("te", lambda e, ii=i, t2=tc2, bb=bankp,
                                   sl=slot, st2=src_t:
                                   e.matmul(ps[bb + t2],
                                            wst[:, sl * 4096 + ii * 128:
                                                sl * 4096 + (ii + 1) * 128],
                                            st2[:, ii * 1024 + t2 * 512:
                                                ii * 1024 + t2 * 512 + 512],
                                            start=(ii == 0), stop=(ii == 31)),
                                   deps=dep)
                        if i == 0:
                            psb[bank].wrote(mm)
                        elif i == 31:
                            psb[bank].also_wrote(mm)
                        mm_last[tc2] = mm
                for tc2 in range(2):
                    wst_b[slot].read(mm_last[tc2])
                    for b in SRC_b:
                        b.read(mm_last[tc2])
                for tc2 in range(2):
                    bank = bankp + tc2
                    hcol = o * 1024 + tc2 * 512
                    o_ev = P.add("sc", lambda e, hc=hcol, oo=o, bb=bank,
                                 d2=dst_t, s3=ssb:
                                 e.activation(d2[:, hc:hc + 512], ps[bb],
                                              AF.Gelu, scale=s3[:, oo:oo + 1]),
                                 deps=[mm_last[tc2], sscl] + DST_b[o].wdeps())
                    psb[bank].read(o_ev)
                    if tc2 == 0:
                        DST_b[o].wrote(o_ev)
                    else:
                        DST_b[o].also_wrote(o_ev)

        h_writers = [w for b in H_b for w in b.writers]

        def emit_reload(fc2, tc2, j2, fen):
            tok0 = fc2 * 1024 + tc2 * 512 + j2 * 128
            rslot = j2 % 2
            o_ar = P.add("sc", lambda e, t0=tok0, rs=rslot: e.dma_start(
                atr[:, rs * 1024:(rs + 1) * 1024], attn_dram[t0:t0 + 128, :]),
                deps=atr_b[rslot].wdeps() + attn_row_writers[tok0 // 128] + fen,
                sem="dAR%d" % rslot, dma=True)
            atr_b[rslot].wrote(o_ar)
            o_xr = P.add("sc", lambda e, t0=tok0, rs=rslot: e.dma_start(
                xrl[:, rs * 1024:(rs + 1) * 1024], x_hbm[t0:t0 + 128, :]),
                deps=xrl_b[rslot].wdeps() + fen, sem="dXR%d" % rslot, dma=True)
            xrl_b[rslot].wrote(o_xr)
            return (o_ar, o_xr)

        for tc2 in range(2):
            reloads = {}
            fen0 = attn_fence if (fc == 0 and tc2 == 0) else []
            reloads[0] = emit_reload(fc, tc2, 0, fen0)
            reloads[1] = emit_reload(fc, tc2, 1, fen0)
            for m in range(8):
                slot = next_wslot()
                wsrc = w3t[m]
                o_w = wstream(slot, wsrc, [], 32)
                bank = m % 2
                mms = []
                for i in range(32):
                    dep = []
                    if i == 0:
                        dep = psb[bank].wdeps() + [o_w] + h_writers
                    mm = P.add("te", lambda e, ii=i, t2=tc2, bb=bank, sl=slot:
                               e.matmul(ps[bb],
                                        wst[:, sl * 4096 + ii * 128:
                                            sl * 4096 + (ii + 1) * 128],
                                        H_sb[:, ii * 1024 + t2 * 512:
                                             ii * 1024 + t2 * 512 + 512],
                                        start=(ii == 0), stop=(ii == 31)),
                               deps=dep)
                    mms.append(mm)
                wst_b[slot].read(mms[-1])
                for b in H_b:
                    b.read(mms[-1])
                psb[bank].wrote(mms[-1])
                eslot = m % 2
                o_ev = P.add("sc", lambda e, m2=m, bb=bank, es2=eslot:
                             e.activation(ffnE[:, es2 * 512:(es2 + 1) * 512],
                                          ps[bb], AF.Copy,
                                          scale=s3_sb[:, m2:m2 + 1]),
                             deps=[mms[-1], scl["s3"]] + ffnE_b[eslot].wdeps())
                psb[bank].read(o_ev)
                ffnE_b[eslot].wrote(o_ev)
                for j2 in range(4):
                    tbank = 4 + (j2 % 2)
                    o_tr = P.add("te", lambda e, es2=eslot, jj=j2, tb=tbank:
                                 e.transpose(ps[tb][:, 0:128],
                                             ffnE[:, es2 * 512 + jj * 128:
                                                  es2 * 512 + (jj + 1) * 128],
                                             identf[:, :]),
                                 deps=[o_ev, ld_idf] + psb[tbank].wdeps())
                    ffnE_b[eslot].read(o_tr)
                    psb[tbank].wrote(o_tr)
                    o_cp = P.add("vx", lambda e, jj=j2, m2=m, tb=tbank:
                                 e.tensor_copy(ost[:, jj * 1024 + m2 * 128:
                                                   jj * 1024 + (m2 + 1) * 128],
                                               ps[tb][:, 0:128]),
                                 deps=[o_tr]
                                 + (ost_b[j2].wdeps() if m == 0 else []))
                    psb[tbank].read(o_cp)
                    if m == 0:
                        ost_b[j2].wrote(o_cp)
                    else:
                        ost_b[j2].also_wrote(o_cp)
            for j2 in range(4):
                tok0 = fc * 1024 + tc2 * 512 + j2 * 128
                rslot = j2 % 2
                o_ar, o_xr = reloads[j2]
                o_a1 = P.add("vx", lambda e, jj=j2, rs=rslot: e.tensor_add(
                    ost[:, jj * 1024:(jj + 1) * 1024],
                    ost[:, jj * 1024:(jj + 1) * 1024],
                    atr[:, rs * 1024:(rs + 1) * 1024]),
                    deps=[o_ar] + ost_b[j2].wdeps())
                atr_b[rslot].read(o_a1)
                ost_b[j2].also_wrote(o_a1)
                o_a2 = P.add("vx", lambda e, jj=j2, rs=rslot: e.tensor_add(
                    ost[:, jj * 1024:(jj + 1) * 1024],
                    ost[:, jj * 1024:(jj + 1) * 1024],
                    xrl[:, rs * 1024:(rs + 1) * 1024]),
                    deps=[o_xr, o_a1])
                xrl_b[rslot].read(o_a2)
                ost_b[j2].also_wrote(o_a2)
                if j2 + 2 < 4:
                    reloads[j2 + 2] = emit_reload(fc, tc2, j2 + 2, [])
                o_st = P.add("sy", lambda e, t0=tok0, jj=j2: e.dma_start(
                    out_hbm[t0:t0 + 128, :], ost[:, jj * 1024:(jj + 1) * 1024]),
                    deps=[o_a2], sem="dO", dma=True)
                ost_b[j2].read(o_st)

    last_o = [op for op in P.ops["sy"] if op.sem == "dO"][-1]
    P.add("sy", lambda e: e.nop(), deps=[last_o])

    P.finalize_and_emit()
    es3.close()
    esg.close()
    return nc


def _prep_inputs(ins):
    f = np.float32
    Wq, Wk, Wv, Wo = (np.asarray(ins["L0_attn_Wq"], f), np.asarray(ins["L0_attn_Wk"], f),
                      np.asarray(ins["L0_attn_Wv"], f), np.asarray(ins["L0_attn_Wo"], f))
    def stream128(wT):
        nI = wT.shape[0] // 128
        return np.ascontiguousarray(
            wT.reshape(nI, 128, wT.shape[1]).transpose(1, 0, 2).reshape(
                128, nI * wT.shape[1]))

    def streamO(wT, ncols):
        nI = wT.shape[0] // 128
        nO = wT.shape[1] // ncols
        a = wT.reshape(nI, 128, nO, ncols).transpose(2, 1, 0, 3)
        return np.ascontiguousarray(a.reshape(nO, 128, nI * ncols))

    d = {}
    d["wqt"] = stream128(np.ascontiguousarray(Wq.T))
    kt = np.empty((D, D), f)
    for c in range(8):
        blk = Wk.T[:, c * 64:(c + 1) * 64]
        kt[:, c * 128:c * 128 + 64] = blk
        kt[:, c * 128 + 64:(c + 1) * 128] = blk
    d["wkt"] = stream128(kt)
    vt = np.empty((D, D), f)
    for h in range(8):
        blk = Wv.T[:, h * 64:(h + 1) * 64]
        vt[:, h * 128:h * 128 + 64] = blk
        vt[:, h * 128 + 64:(h + 1) * 128] = blk
    d["wvt"] = stream128(vt)
    d["wot"] = stream128(np.ascontiguousarray(Wo.T))
    d["bo_bc"] = np.ascontiguousarray(
        np.broadcast_to(np.asarray(ins["L0_attn_bo"], f)[None, :], (128, D)))
    d["w1t"] = streamO(np.asarray(ins["L0_ffn_w1_w"], f).T, 128)
    d["w2t"] = streamO(np.asarray(ins["L0_ffn_w2_w"], f).T, 128)
    d["h1t"] = streamO(np.asarray(ins["L0_ffn_h1_w"], f).T, 128)
    d["h2t"] = streamO(np.asarray(ins["L0_ffn_h2_w"], f).T, 128)
    d["w3t"] = streamO(np.asarray(ins["L0_ffn_w3_w"], f).T, 128)
    d["s1r"] = np.ascontiguousarray(np.asarray(ins["L0_ffn_w1_s"], f).reshape(32, 128).T)
    d["s2r"] = np.ascontiguousarray(np.asarray(ins["L0_ffn_w2_s"], f).reshape(32, 128).T)
    d["sh1r"] = np.ascontiguousarray(np.asarray(ins["L0_ffn_h1_s"], f).reshape(32, 128).T)
    d["sh2r"] = np.ascontiguousarray(np.asarray(ins["L0_ffn_h2_s"], f).reshape(32, 128).T)
    d["s3r"] = np.ascontiguousarray(np.asarray(ins["L0_ffn_w3_s"], f).reshape(8, 128).T)
    p = np.arange(128, dtype=f)[:, None]
    u = np.arange(3968, dtype=f)[None, :]
    dst0 = np.abs(p + 1920.0 - u)
    d["dist"] = np.concatenate([dst0 / 64.0, dst0 / 128.0], axis=1).astype(np.float16)
    si = np.zeros((128, 8 * 128), np.float16)
    for hp2 in range(8):
        np.fill_diagonal(si[:, hp2 * 128:(hp2 + 1) * 128], -(2.0 ** (5 - 2 * hp2)))
    d["slopei"] = si
    d["identf"] = np.eye(128, dtype=f)
    sel = np.zeros((128, 256), f)
    sel[64, 0:64] = 1.0
    sel[0, 128 + 64:256] = 1.0
    d["selrow"] = sel
    return d


def kernel(**inputs):
    x = np.asarray(inputs["L0_attn_x"], dtype=np.float32)
    shared = _prep_inputs(inputs)
    nc = build_program()
    in_maps = []
    for c in range(N_CORES):
        m = dict(shared)
        m["x"] = np.ascontiguousarray(x[2 * c:2 * c + 2].reshape(T, D))
        in_maps.append(m)
    res = run_bass_kernel_spmd(nc, in_maps, core_ids=list(range(N_CORES)))
    outs = [res.results[c]["out"].reshape(2, S, D) for c in range(N_CORES)]
    return np.ascontiguousarray(np.concatenate(outs, axis=0))




# version 15
# speedup vs baseline: 1.0199x; 1.0199x over previous
import numpy as np
import concourse.bass as bass
import concourse.mybir as mybir
from concourse.bass_utils import run_bass_kernel_spmd
from contextlib import ExitStack

F32 = mybir.dt.float32
F16 = mybir.dt.float16
BF16 = mybir.dt.bfloat16
AF = mybir.ActivationFunctionType
ALU = mybir.AluOpType
AX = mybir.AxisListType

N_CORES = 8
S = 2048
T = 2 * S
D = 1024
HID = 4096
NKT = 16
NQC = 4
FC = 1024
NFC = T // FC
EXP_SHIFT = -32.0
EPS = 1e-5
DW = 2432

ENGS = ("te", "vx", "sc", "gp", "sy")


class Op:
    __slots__ = ("eng", "fn", "sem", "amt", "gidx", "cum", "need_inc", "deps",
                 "dma")


class Prog:

    def __init__(self, nc):
        self.nc = nc
        self.ops = {e: [] for e in ENGS}
        self.all = []
        self.dma_sem_owner = {}

    def add(self, eng, fn, deps=(), sem=None, dma=False):
        op = Op()
        op.eng = eng
        op.fn = fn
        op.dma = dma
        op.sem = sem if sem is not None else eng
        if dma:
            assert sem is not None
            assert self.dma_sem_owner.setdefault(sem, eng) == eng
        op.amt = 16 if dma else 1
        op.deps = [d for d in deps if d is not None]
        op.gidx = len(self.all)
        op.need_inc = dma
        op.cum = 0
        for d in op.deps:
            assert d.gidx < op.gidx
        self.ops[eng].append(op)
        self.all.append(op)
        return op

    def fence(self):
        return [self.ops[e][-1] for e in ENGS if self.ops[e]]

    def finalize_and_emit(self):
        for op in self.all:
            for d in op.deps:
                if d.dma or d.eng != op.eng:
                    d.need_inc = True
        counters = {}
        for e in ENGS:
            for op in self.ops[e]:
                if op.need_inc:
                    counters[op.sem] = counters.get(op.sem, 0) + op.amt
                    op.cum = counters[op.sem]
        nc = self.nc
        handles = {n: nc.alloc_semaphore(name="sm_" + n) for n in counters}
        prog = self

        def emitter(eng_name):
            def f(eng):
                last_wait = {}
                for op in prog.ops[eng_name]:
                    waits = {}
                    for d in op.deps:
                        if (d.eng == op.eng and not d.dma) or not d.need_inc:
                            continue
                        waits[d.sem] = max(waits.get(d.sem, 0), d.cum)
                    for sn in sorted(waits):
                        v = waits[sn]
                        if last_wait.get(sn, -1) < v:
                            eng.wait_ge(handles[sn], v)
                            last_wait[sn] = v
                    inst = op.fn(eng)
                    if op.need_inc:
                        inst.then_inc(handles[op.sem], op.amt)
            return f

        with nc.Block() as block:
            block.tensor(emitter("te"))
            block.vector(emitter("vx"))
            block.scalar(emitter("sc"))
            block.gpsimd(emitter("gp"))
            block.sync(emitter("sy"))


class Buf:

    def __init__(self):
        self.writers = []
        self.readers = []

    def rdeps(self):
        return list(self.writers)

    def wdeps(self):
        return self.readers + self.writers

    def wrote(self, op):
        self.writers = [op]
        self.readers = []

    def also_wrote(self, op):
        self.writers.append(op)

    def read(self, op):
        self.readers.append(op)


def build_program():
    nc = bass.Bass("TRN2", target_bir_lowering=False)
    P = Prog(nc)

    x_hbm = nc.dram_tensor("x", [T, D], F32, kind="ExternalInput").ap()
    wqt = nc.dram_tensor("wqt", [128, 8 * 1024], F32, kind="ExternalInput").ap()
    wkt = nc.dram_tensor("wkt", [128, 8 * 1024], F32, kind="ExternalInput").ap()
    wvt = nc.dram_tensor("wvt", [128, 8 * 1024], F32, kind="ExternalInput").ap()
    wot = nc.dram_tensor("wot", [128, 8 * 1024], F32, kind="ExternalInput").ap()
    bo_in = nc.dram_tensor("bo_bc", [128, D], F32, kind="ExternalInput").ap()
    w1t = nc.dram_tensor("w1t", [32, 128, 8 * 128], F32, kind="ExternalInput").ap()
    w2t = nc.dram_tensor("w2t", [32, 128, 8 * 128], F32, kind="ExternalInput").ap()
    h1t = nc.dram_tensor("h1t", [32, 128, 32 * 128], F32, kind="ExternalInput").ap()
    h2t = nc.dram_tensor("h2t", [32, 128, 32 * 128], F32, kind="ExternalInput").ap()
    w3t = nc.dram_tensor("w3t", [8, 128, 32 * 128], F32, kind="ExternalInput").ap()
    s1_in = nc.dram_tensor("s1r", [128, 32], F32, kind="ExternalInput").ap()
    s2_in = nc.dram_tensor("s2r", [128, 32], F32, kind="ExternalInput").ap()
    sh1_in = nc.dram_tensor("sh1r", [128, 32], F32, kind="ExternalInput").ap()
    sh2_in = nc.dram_tensor("sh2r", [128, 32], F32, kind="ExternalInput").ap()
    s3_in = nc.dram_tensor("s3r", [128, 8], F32, kind="ExternalInput").ap()
    dist_in = nc.dram_tensor("dist", [128, 2 * 3968], F16, kind="ExternalInput").ap()
    idf_in = nc.dram_tensor("identf", [128, 128], F32, kind="ExternalInput").ap()
    sel_in = nc.dram_tensor("selrow", [128, 256], F32, kind="ExternalInput").ap()
    si_in = nc.dram_tensor("slopei", [128, 8 * 128], F16, kind="ExternalInput").ap()
    out_hbm = nc.dram_tensor("out", [T, D], F32, kind="ExternalOutput").ap()
    xn_dram = nc.dram_tensor("xn_scr", [8, 128, 8 * 512], BF16, kind="Internal").ap()
    attn_dram = nc.dram_tensor("attn_scr", [T, D], F32, kind="Internal").ap()

    esg = ExitStack()

    def sbg(name, shape, dtype):
        return esg.enter_context(nc.sbuf_tensor(name, shape, dtype))

    identf = sbg("idf_sb", [128, 128], F32)
    selrow = sbg("sel_sb", [128, 256], BF16)
    s1_sb = sbg("s1_sb", [128, 32], F32)
    s2_sb = sbg("s2_sb", [128, 32], F32)
    sh1_sb = sbg("sh1_sb", [128, 32], F32)
    sh2_sb = sbg("sh2_sb", [128, 32], F32)
    s3_sb = sbg("s3_sb", [128, 8], F32)
    biasC = sbg("biasC", [128, 1], F32)

    pp = [esg.enter_context(nc.psum_tensor("pp%d" % i, [128, 1024], F32))
          for i in range(4)]
    ps = [pp[i // 2][:, (i % 2) * 512:(i % 2) * 512 + 512] for i in range(8)]
    pse_pv = ps[4]
    pso_pv = ps[5]
    ps6_lo = pp[3][0:64, 0:512]
    ps7_hi = pp[3][64:128, 512:1024]
    psb = [Buf() for _ in range(8)]

    ld_idf = P.add("gp", lambda e: e.dma_start(identf[:, :], idf_in),
                   sem="dC", dma=True)
    ld_sel = P.add("gp", lambda e: e.dma_start(selrow[:, :], sel_in),
                   sem="dC", dma=True)
    scl = {}
    for nm, src, dst in (("s1", s1_in, s1_sb), ("s2", s2_in, s2_sb),
                         ("sh1", sh1_in, sh1_sb), ("sh2", sh2_in, sh2_sb),
                         ("s3", s3_in, s3_sb)):
        scl[nm] = P.add("sy", lambda e, d=dst, s=src: e.dma_start(d[:, :], s),
                        sem="dCs", dma=True)
    op_biasC = P.add("gp", lambda e: e.memset(biasC[:, :], EXP_SHIFT))

    xn_grp_writers = [[] for _ in range(8)]
    attn_row_writers = [[] for _ in range(32)]

    es_w = ExitStack()

    def sbw(name, shape, dtype):
        return es_w.enter_context(nc.sbuf_tensor(name, shape, dtype))

    wq_sb = sbw("wq_sb", [128, 8 * 1024], BF16)
    wo_sb = sbw("wo_sb", [128, 8 * 1024], BF16)
    wk_sb = sbw("wk_sb", [128, 8 * 1024], BF16)
    wv_sb = sbw("wv_sb", [128, 8 * 1024], BF16)
    K_dup = sbw("K_dup", [128, 8 * 2048], BF16)
    V_sb = sbw("V_sb", [128, 16 * 8 * 192], BF16)
    xn_at = sbw("xn_at", [128, 2 * 8 * 512], BF16)

    K_b = [[Buf() for _ in range(NKT)] for _ in range(8)]
    V_b = [[Buf() for _ in range(NKT)] for _ in range(8)]
    xn_at_b = [Buf(), Buf()]
    Q_b = [Buf() for _ in range(8)]
    P_b = [Buf() for _ in range(4)]
    PV_b = [Buf() for _ in range(8)]
    sums_b = [Buf() for _ in range(8)]
    bc_b = Buf()
    ast_b = Buf()
    dist_b = Buf()

    def wload(dst_t, src):
        return P.add("gp", lambda e, d=dst_t, s=src: e.dma_start(d[:, :], s),
                     sem="dC", dma=True)

    ld_wq = wload(wq_sb, wqt)
    ld_wo = wload(wo_sb, wot)
    ld_wk = wload(wk_sb, wkt)
    ld_wv = wload(wv_sb, wvt)

    op_vz = P.add("gp", lambda e: e.memset(V_sb[:, :], 0.0))
    ones_ap = bass.AP(V_sb[:, :].tensor, 64,
                      [[16 * 8 * 192, 128], [192, 16 * 8], [1, 1]])
    op_vo = P.add("gp", lambda e: e.memset(ones_ap, 1.0), deps=[op_vz])
    v_init = [op_vo]

    def xn_slot_load(slot, scol, extra):
        dst = xn_at[:, slot * 4096:(slot + 1) * 4096]
        g = scol // 512
        op = P.add("sy", lambda e, d=dst, s=xn_dram[g]: e.dma_start(d, s),
                   deps=xn_at_b[slot].wdeps() + xn_grp_writers[g] + extra,
                   sem="dX%d" % slot, dma=True)
        xn_at_b[slot].wrote(op)
        return op


    def kv_chunk(s, j):
        slot = j % 2
        o_x = xn_slot_load(slot, s * S + j * 512, [])
        xslot = xn_at[:, slot * 4096:(slot + 1) * 4096]
        for c in range(8):
            bank = c % 4
            mms = []
            for i in range(8):
                dep = (psb[bank].wdeps() + [ld_wk, o_x]) if i == 0 else []
                mm = P.add("te", lambda e, ii=i, cc=c, bb=bank, xs=xslot:
                           e.matmul(ps[bb],
                                    wk_sb[:, ii * 1024 + cc * 128:
                                          ii * 1024 + (cc + 1) * 128],
                                    xs[:, ii * 512:(ii + 1) * 512],
                                    start=(ii == 0), stop=(ii == 7)),
                           deps=dep)
                mms.append(mm)
            xn_at_b[slot].read(mms[-1])
            psb[bank].wrote(mms[-1])
            kt0 = j * 4
            kdeps = []
            for kk in range(4):
                kdeps += K_b[c][kt0 + kk].wdeps()
            dst = K_dup[:, c * 2048 + j * 512:c * 2048 + (j + 1) * 512]
            o_ev = P.add("vx", lambda e, d=dst, bb=bank: e.tensor_copy(
                d, ps[bb]), deps=[mms[-1]] + kdeps)
            psb[bank].read(o_ev)
            for kk in range(4):
                K_b[c][kt0 + kk].wrote(o_ev)
        for tt4 in range(4):
            kt = j * 4 + tt4
            for half in range(2):
                bank = (tt4 * 2 + half) % 4
                mms = []
                for i in range(8):
                    dep = (psb[bank].wdeps() + [ld_wv, o_x] + v_init) \
                        if i == 0 else []
                    mm = P.add("te", lambda e, ii=i, hh=half, t4=tt4,
                               bb=bank, xs=xslot:
                               e.matmul(ps[bb],
                                        xs[:, ii * 512 + t4 * 128:
                                           ii * 512 + (t4 + 1) * 128],
                                        wv_sb[:, ii * 1024 + hh * 512:
                                              ii * 1024 + (hh + 1) * 512],
                                        start=(ii == 0), stop=(ii == 7)),
                               deps=dep)
                    mms.append(mm)
                xn_at_b[slot].read(mms[-1])
                psb[bank].wrote(mms[-1])
                for h2 in range(4):
                    kvh = half * 4 + h2
                    blk = (kt * 8 + kvh) * 192
                    vb = V_b[kvh][kt]
                    dst = bass.AP(V_sb[:, :].tensor, blk,
                                  [[16 * 8 * 192, 128], [128, 2], [1, 64]])
                    o_ev = P.add("vx", lambda e, d=dst, bb=bank, hx=h2:
                                 e.tensor_copy(
                                     d, ps[bb][:, hx * 128:(hx + 1) * 128]),
                                 deps=[mms[-1]] + vb.wdeps() + v_init)
                    psb[bank].read(o_ev)
                    vb.wrote(o_ev)

    es1 = ExitStack()

    def sb1(name, shape, dtype):
        return es1.enter_context(nc.sbuf_tensor(name, shape, dtype))

    x_tile = sb1("x_tile", [128, 3 * D], F32)
    x_tile_b = [Buf() for _ in range(3)]
    xn_tm = sb1("xn_tm", [128, 2 * D], F32)
    xn_tm_b = [Buf(), Buf()]
    sq_scr = sb1("sq_scr", [128, 2 * D], BF16)
    sq_b = [Buf(), Buf()]
    stat = sb1("stat", [128, 24], F32)
    stat_b = [Buf() for _ in range(4)]
    xn_stage = sb1("xn_stage", [128, 2 * 4096], BF16)
    xn_stage_b = [[Buf() for _ in range(8)] for _ in range(2)]

    for tt in range(32):
        sl3, sl2, sl4 = tt % 3, tt % 2, tt % 4
        g = tt // 4
        gs = g % 2
        xa = x_tile[:, sl3 * D:(sl3 + 1) * D]
        xb = x_tile_b[sl3]
        ldx = P.add("sy", lambda e, d=xa, o=tt: e.dma_start(
            d, x_hbm[o * 128:(o + 1) * 128, :]),
            deps=xb.wdeps(), sem="dT%d" % sl3, dma=True)
        xb.wrote(ldx)
        st = stat[:, sl4 * 6:(sl4 + 1) * 6]
        sbu = stat_b[sl4]
        o_sum = P.add("vx", lambda e, s=xa, c=st: e.reduce_sum(
            c[:, 0:1], s, axis=AX.X, opt_input=False, opt_output=False),
            deps=xb.rdeps() + sbu.wdeps())
        xb.read(o_sum)
        sbu.wrote(o_sum)
        o_sq = P.add("sc", lambda e, s=xa, c=st,
                     q=sq_scr[:, sl2 * D:(sl2 + 1) * D]: e.activation(
                         q, s, AF.Square, accum_out=c[:, 1:2]),
                     deps=xb.rdeps() + [o_sum] + sq_b[sl2].wdeps())
        xb.read(o_sq)
        sq_b[sl2].wrote(o_sq)
        sbu.also_wrote(o_sq)
        o_mu = P.add("vx", lambda e, c=st: e.tensor_scalar_mul(
            c[:, 2:3], c[:, 0:1], 1.0 / D), deps=[o_sum, o_sq])
        o_ex2 = P.add("vx", lambda e, c=st: e.tensor_scalar_mul(
            c[:, 3:4], c[:, 1:2], 1.0 / D), deps=[o_sq])
        o_nmu2 = P.add("vx", lambda e, c=st: e.scalar_tensor_tensor(
            c[:, 4:5], c[:, 2:3], -1.0, c[:, 2:3], op0=ALU.mult, op1=ALU.mult),
            deps=[o_mu])
        o_var = P.add("vx", lambda e, c=st: e.tensor_add(
            c[:, 3:4], c[:, 3:4], c[:, 4:5]), deps=[o_ex2, o_nmu2])
        o_veps = P.add("vx", lambda e, c=st: e.tensor_scalar_add(
            c[:, 3:4], c[:, 3:4], EPS), deps=[o_var])
        o_std = P.add("sc", lambda e, c=st: e.activation(
            c[:, 4:5], c[:, 3:4], AF.Sqrt), deps=[o_veps])
        o_rstd = P.add("vx", lambda e, c=st: e.reciprocal(
            c[:, 5:6], c[:, 4:5]), deps=[o_std])
        for o in (o_mu, o_ex2, o_nmu2, o_var, o_veps, o_std, o_rstd):
            sbu.also_wrote(o)
        o_gate = P.add("sc", lambda e: e.nop(), deps=[o_rstd, o_mu])
        xt = xn_tm[:, sl2 * D:(sl2 + 1) * D]
        o_norm = P.add("vx", lambda e, s=xa, d=xt, c=st: e.tensor_scalar(
            d, s, c[:, 2:3], c[:, 5:6], op0=ALU.subtract, op1=ALU.mult),
            deps=xb.rdeps() + [o_gate] + xn_tm_b[sl2].wdeps())
        xb.read(o_norm)
        sbu.read(o_norm)
        xn_tm_b[sl2].wrote(o_norm)
        for dch in range(8):
            bank = 6 + (dch % 2)
            o_tr = P.add("te", lambda e, s=xt[:, dch * 128:(dch + 1) * 128],
                         bb=bank: e.transpose(ps[bb][:, 0:128], s, identf[:, :]),
                         deps=[ld_idf, o_norm] + psb[bank].wdeps())
            xn_tm_b[sl2].read(o_tr)
            psb[bank].wrote(o_tr)
            stb = xn_stage_b[gs][dch]
            dst = xn_stage[:, gs * 4096 + dch * 512 + sl4 * 128:
                           gs * 4096 + dch * 512 + (sl4 + 1) * 128]
            o_ev = P.add("vx", lambda e, d=dst, bb=bank: e.tensor_copy(
                d, ps[bb][:, 0:128]),
                deps=[o_tr] + (stb.wdeps() if sl4 == 0 else []))
            psb[bank].read(o_ev)
            if sl4 == 0:
                stb.wrote(o_ev)
            else:
                stb.also_wrote(o_ev)
        if sl4 == 3:
            for dch in range(8):
                stb = xn_stage_b[gs][dch]
                src = xn_stage[:, gs * 4096 + dch * 512:gs * 4096 + (dch + 1) * 512]
                o_st = P.add("sy", lambda e, s=src, dd=dch, gg=g: e.dma_start(
                    xn_dram[gg][:, dd * 512:(dd + 1) * 512], s),
                    deps=stb.wdeps(), sem="dXS", dma=True)
                stb.read(o_st)
                xn_grp_writers[g].append(o_st)
            if g < 4:
                kv_chunk(0, g)

    ln_fence = P.fence()
    es1.close()

    es2 = ExitStack()

    def sb2(name, shape, dtype):
        return es2.enter_context(nc.sbuf_tensor(name, shape, dtype))

    dist_w = sb2("dist_w", [128, 2 * DW], F16)
    si_sb = sb2("si_sb", [128, 8 * 128], F16)
    bo_sb = sb2("bo_sb", [128, D], F32)
    ld_bo = P.add("sy", lambda e: e.dma_start(bo_sb[:, :], bo_in),
                  deps=ln_fence, sem="dCs", dma=True)
    ld_si = P.add("sy", lambda e: e.dma_start(si_sb[:, :], si_in),
                  deps=ln_fence, sem="dCs", dma=True)
    Q_sb = sb2("Q_sb", [128, 8 * 512], BF16)
    P_sb = sb2("P_sb", [128, 4 * 512], BF16)
    PV_sb = sb2("PV_sb", [128, 8 * 512], BF16)
    sums_sb = sb2("sums_sb", [128, 2 * 512], BF16)
    bc_sb = sb2("bc_sb", [128, 512], F32)
    ast = sb2("ast", [128, D], F32)

    for s in range(2):
        if s == 1:
            for j in range(4):
                kv_chunk(1, j)
        for qc in range(NQC):
            xslot = (s * NQC + qc) % 2
            o_x = xn_slot_load(xslot, s * S + qc * 512, [])
            xs = xn_at[:, xslot * 4096:(xslot + 1) * 4096]
            dsrc = bass.AP(dist_in.tensor, qc * 512,
                           [[2 * 3968, 128], [3968, 2], [1, DW]])
            o_dist = P.add("sy", lambda e, s2=dsrc: e.dma_start(
                dist_w[:, :].rearrange("p (s c) -> p s c", s=2), s2),
                deps=dist_b.wdeps() + (ln_fence if s == 0 and qc == 0 else []),
                sem="dDW", dma=True)
            dist_b.wrote(o_dist)
            for hp in range(8):
                bank = hp % 2
                mms = []
                for i in range(8):
                    dep = (psb[bank].wdeps() + [ld_wq, o_x]) if i == 0 else []
                    mm = P.add("te", lambda e, ii=i, h=hp, bb=bank, x2=xs:
                               e.matmul(ps[bb],
                                        wq_sb[:, ii * 1024 + h * 128:
                                              ii * 1024 + (h + 1) * 128],
                                        x2[:, ii * 512:(ii + 1) * 512],
                                        start=(ii == 0), stop=(ii == 7)),
                               deps=dep)
                    mms.append(mm)
                xn_at_b[xslot].read(mms[-1])
                psb[bank].wrote(mms[-1])
                o_ev = P.add("vx", lambda e, h=hp, bb=bank: e.tensor_copy(
                    Q_sb[:, h * 512:(h + 1) * 512], ps[bb]),
                    deps=[mms[-1]] + Q_b[hp].wdeps()
                    + (ln_fence if (s == 0 and qc == 0) else []))
                psb[bank].read(o_ev)
                Q_b[hp].wrote(o_ev)

            for hp in range(8):
                kvh = hp
                slope_e = float(2.0 ** (-(2 * hp + 1)))
                slope_o = float(2.0 ** (-(2 * hp + 2)))
                pv_e_mm = pv_o_mm = None
                exps = {}

                def emit_scores(kt):
                    bp = kt % 2
                    bank_e, bank_o = bp * 2, bp * 2 + 1
                    uloc = 1920 - 128 * kt
                    kslc = K_dup[:, kvh * 2048 + kt * 128:
                                 kvh * 2048 + (kt + 1) * 128]
                    mm_se = P.add("te", lambda e, ks=kslc, h=hp, bb=bank_e:
                                  e.matmul(ps[bb], ks[0:64, :],
                                           Q_sb[0:64, h * 512:(h + 1) * 512],
                                           start=True, stop=False,
                                           tile_position=(0, 0)),
                                  deps=K_b[kvh][kt].rdeps() + Q_b[hp].rdeps()
                                  + psb[bank_e].wdeps())
                    K_b[kvh][kt].read(mm_se)
                    Q_b[hp].read(mm_se)
                    psb[bank_e].wrote(mm_se)
                    mm_so = P.add("te", lambda e, ks=kslc, h=hp, bb=bank_o:
                                  e.matmul(ps[bb], ks[64:128, :],
                                           Q_sb[64:128, h * 512:(h + 1) * 512],
                                           start=True, stop=False,
                                           tile_position=(64, 0)),
                                  deps=K_b[kvh][kt].rdeps() + Q_b[hp].rdeps()
                                  + psb[bank_o].wdeps())
                    K_b[kvh][kt].read(mm_so)
                    Q_b[hp].read(mm_so)
                    psb[bank_o].wrote(mm_so)
                    mm_be = P.add("te", lambda e, h=hp, u=uloc, bb=bank_e:
                                  e.matmul(ps[bb], si_sb[:, h * 128:(h + 1) * 128],
                                           dist_w[:, u:u + 512],
                                           start=False, stop=True),
                                  deps=[o_dist, ld_si])
                    psb[bank_e].also_wrote(mm_be)
                    dist_b.read(mm_be)
                    mm_bo = P.add("te", lambda e, h=hp, u=uloc, bb=bank_o:
                                  e.matmul(ps[bb], si_sb[:, h * 128:(h + 1) * 128],
                                           dist_w[:, DW + u:DW + u + 512],
                                           start=False, stop=True),
                                  deps=[o_dist, ld_si])
                    psb[bank_o].also_wrote(mm_bo)
                    dist_b.read(mm_bo)
                    ppair = P_sb[:, bp * 1024:(bp + 1) * 1024]
                    expp = P.add("sc", lambda e, d=ppair, b2=bp: e.activation(
                        d, pp[b2][:, :], AF.Exp, bias=biasC[:, :], scale=0.125),
                        deps=[mm_be, mm_bo, op_biasC] + P_b[bp].wdeps())
                    psb[bank_e].read(expp)
                    psb[bank_o].read(expp)
                    P_b[bp].wrote(expp)
                    exps[kt] = (expp, P_sb[:, bp * 1024:bp * 1024 + 512],
                                P_sb[:, bp * 1024 + 512:(bp + 1) * 1024])

                def emit_pv(kt):
                    nonlocal pv_e_mm, pv_o_mm
                    bp = kt % 2
                    expp, pe, po = exps.pop(kt)
                    exp_e = exp_o = expp
                    blk = (kt * 8 + kvh) * 192
                    depe = (psb[4].wdeps() if kt == 0 else []) + [exp_e] \
                        + V_b[kvh][kt].rdeps()
                    pv_e_mm = P.add("te", lambda e, b2=blk, p2=pe, k2=kt:
                                    e.matmul(pse_pv,
                                             V_sb[:, b2:b2 + 128], p2,
                                             start=(k2 == 0), stop=(k2 == NKT - 1)),
                                    deps=depe)
                    V_b[kvh][kt].read(pv_e_mm)
                    P_b[bp].read(pv_e_mm)
                    psb[4].wrote(pv_e_mm) if kt == 0 else psb[4].also_wrote(pv_e_mm)
                    depo = (psb[5].wdeps() if kt == 0 else []) + [exp_o] \
                        + V_b[kvh][kt].rdeps()
                    pv_o_mm = P.add("te", lambda e, b2=blk, p2=po, k2=kt:
                                    e.matmul(pso_pv,
                                             V_sb[:, b2 + 64:b2 + 192], p2,
                                             start=(k2 == 0), stop=(k2 == NKT - 1)),
                                    deps=depo)
                    V_b[kvh][kt].read(pv_o_mm)
                    P_b[bp].read(pv_o_mm)
                    psb[5].wrote(pv_o_mm) if kt == 0 else psb[5].also_wrote(pv_o_mm)

                for kt in range(NKT):
                    emit_scores(kt)
                    if kt >= 1:
                        emit_pv(kt - 1)
                emit_pv(NKT - 1)
                pvb = PV_b[hp]
                smb = sums_b[hp % 2]
                c1 = P.add("vx", lambda e, h=hp: e.tensor_copy(
                    PV_sb[0:64, h * 512:(h + 1) * 512], pse_pv[0:64, :]),
                    deps=[pv_e_mm] + pvb.wdeps())
                psb[4].read(c1)
                c2 = P.add("vx", lambda e, h=hp: e.tensor_copy(
                    sums_sb[64:65, (h % 2) * 512:(h % 2 + 1) * 512], pse_pv[64:65, :]),
                    deps=[pv_e_mm] + smb.wdeps())
                psb[4].read(c2)
                c3 = P.add("vx", lambda e, h=hp: e.tensor_copy(
                    PV_sb[64:128, h * 512:(h + 1) * 512], pso_pv[64:128, :]),
                    deps=[pv_o_mm] + pvb.wdeps())
                psb[5].read(c3)
                c4 = P.add("vx", lambda e, h=hp: e.tensor_copy(
                    sums_sb[0:1, (h % 2) * 512:(h % 2 + 1) * 512], pso_pv[0:1, :]),
                    deps=[pv_o_mm] + smb.wdeps())
                psb[5].read(c4)
                pvb.wrote(c1)
                pvb.also_wrote(c3)
                smb.wrote(c2)
                smb.also_wrote(c4)
                m1 = P.add("te", lambda e, h=hp: e.matmul(
                    ps[6], selrow[64:65, 0:128],
                    sums_sb[64:65, (h % 2) * 512:(h % 2 + 1) * 512],
                    start=True, stop=True),
                    deps=[c2, ld_sel] + psb[6].wdeps())
                psb[6].wrote(m1)
                smb.read(m1)
                m2 = P.add("te", lambda e, h=hp: e.matmul(
                    ps[7], selrow[0:1, 128:256],
                    sums_sb[0:1, (h % 2) * 512:(h % 2 + 1) * 512],
                    start=True, stop=True),
                    deps=[c4, ld_sel] + psb[7].wdeps())
                psb[7].wrote(m2)
                smb.read(m2)
                r1 = P.add("vx", lambda e: e.reciprocal(
                    bc_sb[0:64, :], ps6_lo), deps=[m1] + bc_b.wdeps())
                psb[6].read(r1)
                r2 = P.add("vx", lambda e: e.reciprocal(
                    bc_sb[64:128, :], ps7_hi), deps=[m2] + bc_b.wdeps())
                psb[7].read(r2)
                bc_b.wrote(r1)
                bc_b.also_wrote(r2)
                n1 = P.add("vx", lambda e, h=hp: e.tensor_mul(
                    PV_sb[0:64, h * 512:(h + 1) * 512],
                    PV_sb[0:64, h * 512:(h + 1) * 512], bc_sb[0:64, :]),
                    deps=[r1, c1])
                bc_b.read(n1)
                pvb.also_wrote(n1)
                n2 = P.add("vx", lambda e, h=hp: e.tensor_mul(
                    PV_sb[64:128, h * 512:(h + 1) * 512],
                    PV_sb[64:128, h * 512:(h + 1) * 512], bc_sb[64:128, :]),
                    deps=[r2, c3])
                bc_b.read(n2)
                pvb.also_wrote(n2)

            pv_all_writers = [w for b in PV_b for w in b.writers]
            for j in range(4):
                bank = j % 2
                tok0 = s * S + qc * 512 + j * 128
                evs = []
                for half in range(2):
                    mms = []
                    for h in range(8):
                        dep = []
                        if h == 0:
                            dep = psb[bank].wdeps() + [ld_wo] + pv_all_writers
                        mm = P.add("te", lambda e, h2=h, j2=j, hf=half, bb=bank:
                                   e.matmul(ps[bb],
                                            PV_sb[:, h2 * 512 + j2 * 128:
                                                  h2 * 512 + (j2 + 1) * 128],
                                            wo_sb[:, h2 * 1024 + hf * 512:
                                                  h2 * 1024 + (hf + 1) * 512],
                                            start=(h2 == 0), stop=(h2 == 7)),
                                   deps=dep)
                        mms.append(mm)
                    for b in PV_b:
                        b.read(mms[-1])
                    psb[bank].wrote(mms[-1])
                    dst = ast[:, half * 512:(half + 1) * 512]
                    o_ev = P.add("vx", lambda e, d=dst, hf=half, bb=bank:
                                 e.tensor_add(d, ps[bb],
                                              bo_sb[:, hf * 512:(hf + 1) * 512]),
                                 deps=[mms[-1], ld_bo]
                                 + (ast_b.wdeps() if half == 0 else []))
                    psb[bank].read(o_ev)
                    if half == 0:
                        ast_b.wrote(o_ev)
                    else:
                        ast_b.also_wrote(o_ev)
                    evs.append(o_ev)
                o_st = P.add("sy", lambda e, t0=tok0: e.dma_start(
                    attn_dram[t0:t0 + 128, :], ast[:, :]),
                    deps=evs, sem="dA", dma=True)
                ast_b.read(o_st)
                attn_row_writers[tok0 // 128].append(o_st)

    attn_fence = P.fence()
    es2.close()
    es_w.close()

    es3 = ExitStack()

    def sb3(name, shape, dtype):
        return es3.enter_context(nc.sbuf_tensor(name, shape, dtype))

    H_sb = sb3("H_sb", [128, 32 * 1024], BF16)
    G_sb = sb3("G_sb", [128, 32 * 1024], BF16)
    wst = sb3("wst", [128, 2 * 4096], BF16)
    xn_fc = sb3("xn_fc", [128, 8 * 1024], BF16)
    ffnE = sb3("ffnE", [128, 2 * 512], F32)
    ost = sb3("ost", [128, 4 * 1024], F32)
    atr = sb3("atr", [128, 2 * 1024], F32)
    xrl = sb3("xrl", [128, 2 * 1024], F32)

    H_b = [Buf() for _ in range(32)]
    G_b = [Buf() for _ in range(32)]
    wst_b = [Buf(), Buf()]
    xn_fc_b = Buf()
    ffnE_b = [Buf(), Buf()]
    ost_b = [Buf() for _ in range(4)]
    atr_b = [Buf(), Buf()]
    xrl_b = [Buf(), Buf()]

    wst_ctr = [0]

    def next_wslot():
        s = wst_ctr[0] % 2
        wst_ctr[0] += 1
        return s

    def wstream(slot, src_slice, extra, nrows):
        dst = wst[:, slot * 4096:slot * 4096 + nrows * 128].rearrange(
            "p (i o) -> p i o", i=nrows)
        op = P.add("gp", lambda e, d=dst, s=src_slice: e.dma_start(d, s),
                   deps=wst_b[slot].wdeps() + extra, sem="dW%d" % slot, dma=True)
        wst_b[slot].wrote(op)
        return op

    def emit_xn_load(fc, fence0):
        ops = []
        for g2 in range(2):
            g = 2 * fc + g2
            op = P.add("sc", lambda e, gg=g, g3=g2: e.dma_start(
                xn_fc[:, g3 * 4096:(g3 + 1) * 4096], xn_dram[gg]),
                deps=(xn_fc_b.wdeps() if g2 == 0 else []) + xn_grp_writers[g]
                + fence0, sem="dXF", dma=True)
            ops.append(op)
        xn_fc_b.wrote(ops[0])
        xn_fc_b.also_wrote(ops[1])
        return ops

    for fc in range(NFC):
        if fc == 0:
            o_xns = emit_xn_load(0, attn_fence)
        else:
            o_xns = nxt
        for which in ("w1", "w2"):
            wt = w1t if which == "w1" else w2t
            for o in range(32):
                slot = next_wslot()
                wsrc = wt[o]
                extra = attn_fence if (fc == 0 and which == "w1" and o < 2) else []
                o_w = wstream(slot, wsrc, extra, 8)
                bankp = (o % 2) * 2
                mm_last = [None, None]
                for i in range(8):
                    for tc2 in range(2):
                        bank = bankp + tc2
                        dep = (psb[bank].wdeps() + [o_w] + o_xns) if i == 0 else []
                        mm = P.add("te", lambda e, ii=i, t2=tc2, bb=bankp, sl=slot:
                                   e.matmul(ps[bb + t2],
                                            wst[:, sl * 4096 + ii * 128:
                                                sl * 4096 + (ii + 1) * 128],
                                            xn_fc[:, t2 * 4096 + ii * 512:
                                                  t2 * 4096 + ii * 512 + 512],
                                            start=(ii == 0), stop=(ii == 7)),
                                   deps=dep)
                        if i == 0:
                            psb[bank].wrote(mm)
                        elif i == 7:
                            psb[bank].also_wrote(mm)
                        mm_last[tc2] = mm
                for tc2 in range(2):
                    wst_b[slot].read(mm_last[tc2])
                    xn_fc_b.read(mm_last[tc2])
                for tc2 in range(2):
                    bank = bankp + tc2
                    hcol = o * 1024 + tc2 * 512
                    if which == "w1":
                        o_ev = P.add("sc", lambda e, hc=hcol, oo=o, bb=bank:
                                     e.activation(H_sb[:, hc:hc + 512],
                                                  ps[bb], AF.Silu,
                                                  scale=s1_sb[:, oo:oo + 1]),
                                     deps=[mm_last[tc2], scl["s1"]]
                                     + H_b[o].wdeps())
                    else:
                        o_ev = P.add("vx", lambda e, hc=hcol, oo=o, bb=bank:
                                     e.scalar_tensor_tensor(
                                         H_sb[:, hc:hc + 512], ps[bb],
                                         s2_sb[:, oo:oo + 1],
                                         H_sb[:, hc:hc + 512],
                                         op0=ALU.mult, op1=ALU.mult),
                                     deps=[mm_last[tc2], scl["s2"]]
                                     + (H_b[o].rdeps() if which == "w2" else []))
                    psb[bank].read(o_ev)
                    if which == "w1" and tc2 == 0:
                        H_b[o].wrote(o_ev)
                    else:
                        H_b[o].also_wrote(o_ev)

        if fc + 1 < NFC:
            nxt = emit_xn_load(fc + 1, [])
        for which in ("h1", "h2"):
            ht = h1t if which == "h1" else h2t
            ssb = sh1_sb if which == "h1" else sh2_sb
            sscl = scl["sh1"] if which == "h1" else scl["sh2"]
            SRC_b, DST_b = (H_b, G_b) if which == "h1" else (G_b, H_b)
            src_t, dst_t = (H_sb, G_sb) if which == "h1" else (G_sb, H_sb)
            src_writers = [w for b in SRC_b for w in b.writers]
            for o in range(32):
                slot = next_wslot()
                wsrc = ht[o]
                o_w = wstream(slot, wsrc, [], 32)
                bankp = (o % 2) * 2
                mm_last = [None, None]
                for i in range(32):
                    for tc2 in range(2):
                        bank = bankp + tc2
                        dep = []
                        if i == 0:
                            dep = psb[bank].wdeps() + [o_w] + src_writers
                        mm = P.add("te", lambda e, ii=i, t2=tc2, bb=bankp,
                                   sl=slot, st2=src_t:
                                   e.matmul(ps[bb + t2],
                                            wst[:, sl * 4096 + ii * 128:
                                                sl * 4096 + (ii + 1) * 128],
                                            st2[:, ii * 1024 + t2 * 512:
                                                ii * 1024 + t2 * 512 + 512],
                                            start=(ii == 0), stop=(ii == 31)),
                                   deps=dep)
                        if i == 0:
                            psb[bank].wrote(mm)
                        elif i == 31:
                            psb[bank].also_wrote(mm)
                        mm_last[tc2] = mm
                for tc2 in range(2):
                    wst_b[slot].read(mm_last[tc2])
                    for b in SRC_b:
                        b.read(mm_last[tc2])
                for tc2 in range(2):
                    bank = bankp + tc2
                    hcol = o * 1024 + tc2 * 512
                    o_ev = P.add("sc", lambda e, hc=hcol, oo=o, bb=bank,
                                 d2=dst_t, s3=ssb:
                                 e.activation(d2[:, hc:hc + 512], ps[bb],
                                              AF.Gelu, scale=s3[:, oo:oo + 1]),
                                 deps=[mm_last[tc2], sscl] + DST_b[o].wdeps())
                    psb[bank].read(o_ev)
                    if tc2 == 0:
                        DST_b[o].wrote(o_ev)
                    else:
                        DST_b[o].also_wrote(o_ev)

        h_writers = [w for b in H_b for w in b.writers]

        def emit_reload(fc2, tc2, j2, fen):
            tok0 = fc2 * 1024 + tc2 * 512 + j2 * 128
            rslot = j2 % 2
            o_ar = P.add("sc", lambda e, t0=tok0, rs=rslot: e.dma_start(
                atr[:, rs * 1024:(rs + 1) * 1024], attn_dram[t0:t0 + 128, :]),
                deps=atr_b[rslot].wdeps() + attn_row_writers[tok0 // 128] + fen,
                sem="dAR%d" % rslot, dma=True)
            atr_b[rslot].wrote(o_ar)
            o_xr = P.add("sc", lambda e, t0=tok0, rs=rslot: e.dma_start(
                xrl[:, rs * 1024:(rs + 1) * 1024], x_hbm[t0:t0 + 128, :]),
                deps=xrl_b[rslot].wdeps() + fen, sem="dXR%d" % rslot, dma=True)
            xrl_b[rslot].wrote(o_xr)
            return (o_ar, o_xr)

        for tc2 in range(2):
            reloads = {}
            fen0 = attn_fence if (fc == 0 and tc2 == 0) else []
            reloads[0] = emit_reload(fc, tc2, 0, fen0)
            reloads[1] = emit_reload(fc, tc2, 1, fen0)
            for m in range(8):
                slot = next_wslot()
                wsrc = w3t[m]
                o_w = wstream(slot, wsrc, [], 32)
                bank = m % 2
                mms = []
                for i in range(32):
                    dep = []
                    if i == 0:
                        dep = psb[bank].wdeps() + [o_w] + h_writers
                    mm = P.add("te", lambda e, ii=i, t2=tc2, bb=bank, sl=slot:
                               e.matmul(ps[bb],
                                        wst[:, sl * 4096 + ii * 128:
                                            sl * 4096 + (ii + 1) * 128],
                                        H_sb[:, ii * 1024 + t2 * 512:
                                             ii * 1024 + t2 * 512 + 512],
                                        start=(ii == 0), stop=(ii == 31)),
                               deps=dep)
                    mms.append(mm)
                wst_b[slot].read(mms[-1])
                for b in H_b:
                    b.read(mms[-1])
                psb[bank].wrote(mms[-1])
                eslot = m % 2
                o_ev = P.add("sc", lambda e, m2=m, bb=bank, es2=eslot:
                             e.activation(ffnE[:, es2 * 512:(es2 + 1) * 512],
                                          ps[bb], AF.Copy,
                                          scale=s3_sb[:, m2:m2 + 1]),
                             deps=[mms[-1], scl["s3"]] + ffnE_b[eslot].wdeps())
                psb[bank].read(o_ev)
                ffnE_b[eslot].wrote(o_ev)
                for j2 in range(4):
                    tbank = 4 + (j2 % 2)
                    o_tr = P.add("te", lambda e, es2=eslot, jj=j2, tb=tbank:
                                 e.transpose(ps[tb][:, 0:128],
                                             ffnE[:, es2 * 512 + jj * 128:
                                                  es2 * 512 + (jj + 1) * 128],
                                             identf[:, :]),
                                 deps=[o_ev, ld_idf] + psb[tbank].wdeps())
                    ffnE_b[eslot].read(o_tr)
                    psb[tbank].wrote(o_tr)
                    o_cp = P.add("vx", lambda e, jj=j2, m2=m, tb=tbank:
                                 e.tensor_copy(ost[:, jj * 1024 + m2 * 128:
                                                   jj * 1024 + (m2 + 1) * 128],
                                               ps[tb][:, 0:128]),
                                 deps=[o_tr]
                                 + (ost_b[j2].wdeps() if m == 0 else []))
                    psb[tbank].read(o_cp)
                    if m == 0:
                        ost_b[j2].wrote(o_cp)
                    else:
                        ost_b[j2].also_wrote(o_cp)
            for j2 in range(4):
                tok0 = fc * 1024 + tc2 * 512 + j2 * 128
                rslot = j2 % 2
                o_ar, o_xr = reloads[j2]
                o_a1 = P.add("vx", lambda e, jj=j2, rs=rslot: e.tensor_add(
                    ost[:, jj * 1024:(jj + 1) * 1024],
                    ost[:, jj * 1024:(jj + 1) * 1024],
                    atr[:, rs * 1024:(rs + 1) * 1024]),
                    deps=[o_ar] + ost_b[j2].wdeps())
                atr_b[rslot].read(o_a1)
                ost_b[j2].also_wrote(o_a1)
                o_a2 = P.add("vx", lambda e, jj=j2, rs=rslot: e.tensor_add(
                    ost[:, jj * 1024:(jj + 1) * 1024],
                    ost[:, jj * 1024:(jj + 1) * 1024],
                    xrl[:, rs * 1024:(rs + 1) * 1024]),
                    deps=[o_xr, o_a1])
                xrl_b[rslot].read(o_a2)
                ost_b[j2].also_wrote(o_a2)
                if j2 + 2 < 4:
                    reloads[j2 + 2] = emit_reload(fc, tc2, j2 + 2, [])
                o_st = P.add("sy", lambda e, t0=tok0, jj=j2: e.dma_start(
                    out_hbm[t0:t0 + 128, :], ost[:, jj * 1024:(jj + 1) * 1024]),
                    deps=[o_a2], sem="dO", dma=True)
                ost_b[j2].read(o_st)

    last_o = [op for op in P.ops["sy"] if op.sem == "dO"][-1]
    P.add("sy", lambda e: e.nop(), deps=[last_o])

    P.finalize_and_emit()
    es3.close()
    esg.close()
    return nc


def _prep_inputs(ins):
    f = np.float32
    Wq, Wk, Wv, Wo = (np.asarray(ins["L0_attn_Wq"], f), np.asarray(ins["L0_attn_Wk"], f),
                      np.asarray(ins["L0_attn_Wv"], f), np.asarray(ins["L0_attn_Wo"], f))
    def stream128(wT):
        nI = wT.shape[0] // 128
        return np.ascontiguousarray(
            wT.reshape(nI, 128, wT.shape[1]).transpose(1, 0, 2).reshape(
                128, nI * wT.shape[1]))

    def streamO(wT, ncols):
        nI = wT.shape[0] // 128
        nO = wT.shape[1] // ncols
        a = wT.reshape(nI, 128, nO, ncols).transpose(2, 1, 0, 3)
        return np.ascontiguousarray(a.reshape(nO, 128, nI * ncols))

    d = {}
    d["wqt"] = stream128(np.ascontiguousarray(Wq.T))
    kt = np.empty((D, D), f)
    for c in range(8):
        blk = Wk.T[:, c * 64:(c + 1) * 64]
        kt[:, c * 128:c * 128 + 64] = blk
        kt[:, c * 128 + 64:(c + 1) * 128] = blk
    d["wkt"] = stream128(kt)
    vt = np.empty((D, D), f)
    for h in range(8):
        blk = Wv.T[:, h * 64:(h + 1) * 64]
        vt[:, h * 128:h * 128 + 64] = blk
        vt[:, h * 128 + 64:(h + 1) * 128] = blk
    d["wvt"] = stream128(vt)
    d["wot"] = stream128(np.ascontiguousarray(Wo.T))
    d["bo_bc"] = np.ascontiguousarray(
        np.broadcast_to(np.asarray(ins["L0_attn_bo"], f)[None, :], (128, D)))
    d["w1t"] = streamO(np.asarray(ins["L0_ffn_w1_w"], f).T, 128)
    d["w2t"] = streamO(np.asarray(ins["L0_ffn_w2_w"], f).T, 128)
    d["h1t"] = streamO(np.asarray(ins["L0_ffn_h1_w"], f).T, 128)
    d["h2t"] = streamO(np.asarray(ins["L0_ffn_h2_w"], f).T, 128)
    d["w3t"] = streamO(np.asarray(ins["L0_ffn_w3_w"], f).T, 128)
    d["s1r"] = np.ascontiguousarray(np.asarray(ins["L0_ffn_w1_s"], f).reshape(32, 128).T)
    d["s2r"] = np.ascontiguousarray(np.asarray(ins["L0_ffn_w2_s"], f).reshape(32, 128).T)
    d["sh1r"] = np.ascontiguousarray(np.asarray(ins["L0_ffn_h1_s"], f).reshape(32, 128).T)
    d["sh2r"] = np.ascontiguousarray(np.asarray(ins["L0_ffn_h2_s"], f).reshape(32, 128).T)
    d["s3r"] = np.ascontiguousarray(np.asarray(ins["L0_ffn_w3_s"], f).reshape(8, 128).T)
    p = np.arange(128, dtype=f)[:, None]
    u = np.arange(3968, dtype=f)[None, :]
    dst0 = np.abs(p + 1920.0 - u)
    d["dist"] = np.concatenate([dst0 / 64.0, dst0 / 128.0], axis=1).astype(np.float16)
    si = np.zeros((128, 8 * 128), np.float16)
    for hp2 in range(8):
        np.fill_diagonal(si[:, hp2 * 128:(hp2 + 1) * 128], -(2.0 ** (5 - 2 * hp2)))
    d["slopei"] = si
    d["identf"] = np.eye(128, dtype=f)
    sel = np.zeros((128, 256), f)
    sel[64, 0:64] = 1.0
    sel[0, 128 + 64:256] = 1.0
    d["selrow"] = sel
    return d


def kernel(**inputs):
    x = np.asarray(inputs["L0_attn_x"], dtype=np.float32)
    shared = _prep_inputs(inputs)
    nc = build_program()
    in_maps = []
    for c in range(N_CORES):
        m = dict(shared)
        m["x"] = np.ascontiguousarray(x[2 * c:2 * c + 2].reshape(T, D))
        in_maps.append(m)
    res = run_bass_kernel_spmd(nc, in_maps, core_ids=list(range(N_CORES)))
    outs = [res.results[c]["out"].reshape(2, S, D) for c in range(N_CORES)]
    return np.ascontiguousarray(np.concatenate(outs, axis=0))




# version 16
# speedup vs baseline: 1.1325x; 1.1104x over previous
import numpy as np
import concourse.bass as bass
import concourse.mybir as mybir
from concourse.bass_utils import run_bass_kernel_spmd
from contextlib import ExitStack

F32 = mybir.dt.float32
F16 = mybir.dt.float16
BF16 = mybir.dt.bfloat16
AF = mybir.ActivationFunctionType
ALU = mybir.AluOpType
AX = mybir.AxisListType

N_CORES = 8
S = 2048
T = 2 * S
D = 1024
HID = 4096
NKT = 16
NQC = 4
FC = 1024
NFC = T // FC
EXP_SHIFT = -32.0
EPS = 1e-5
DW = 2432

ENGS = ("te", "vx", "sc", "gp", "sy")


class Op:
    __slots__ = ("eng", "fn", "sem", "amt", "gidx", "cum", "need_inc", "deps",
                 "dma")


class Prog:

    def __init__(self, nc):
        self.nc = nc
        self.ops = {e: [] for e in ENGS}
        self.all = []
        self.dma_sem_owner = {}

    def add(self, eng, fn, deps=(), sem=None, dma=False):
        op = Op()
        op.eng = eng
        op.fn = fn
        op.dma = dma
        op.sem = sem if sem is not None else eng
        if dma:
            assert sem is not None
            assert self.dma_sem_owner.setdefault(sem, eng) == eng
        op.amt = 16 if dma else 1
        op.deps = [d for d in deps if d is not None]
        op.gidx = len(self.all)
        op.need_inc = dma
        op.cum = 0
        for d in op.deps:
            assert d.gidx < op.gidx
        self.ops[eng].append(op)
        self.all.append(op)
        return op

    def fence(self):
        return [self.ops[e][-1] for e in ENGS if self.ops[e]]

    def finalize_and_emit(self):
        for op in self.all:
            for d in op.deps:
                if d.dma or d.eng != op.eng:
                    d.need_inc = True
        counters = {}
        for e in ENGS:
            for op in self.ops[e]:
                if op.need_inc:
                    counters[op.sem] = counters.get(op.sem, 0) + op.amt
                    op.cum = counters[op.sem]
        nc = self.nc
        handles = {n: nc.alloc_semaphore(name="sm_" + n) for n in counters}
        prog = self

        def emitter(eng_name):
            def f(eng):
                last_wait = {}
                for op in prog.ops[eng_name]:
                    waits = {}
                    for d in op.deps:
                        if (d.eng == op.eng and not d.dma) or not d.need_inc:
                            continue
                        waits[d.sem] = max(waits.get(d.sem, 0), d.cum)
                    for sn in sorted(waits):
                        v = waits[sn]
                        if last_wait.get(sn, -1) < v:
                            eng.wait_ge(handles[sn], v)
                            last_wait[sn] = v
                    inst = op.fn(eng)
                    if op.need_inc:
                        inst.then_inc(handles[op.sem], op.amt)
            return f

        with nc.Block() as block:
            block.tensor(emitter("te"))
            block.vector(emitter("vx"))
            block.scalar(emitter("sc"))
            block.gpsimd(emitter("gp"))
            block.sync(emitter("sy"))


class Buf:

    def __init__(self):
        self.writers = []
        self.readers = []

    def rdeps(self):
        return list(self.writers)

    def wdeps(self):
        return self.readers + self.writers

    def wrote(self, op):
        self.writers = [op]
        self.readers = []

    def also_wrote(self, op):
        self.writers.append(op)

    def read(self, op):
        self.readers.append(op)


def build_program():
    nc = bass.Bass("TRN2", target_bir_lowering=False)
    P = Prog(nc)

    x_hbm = nc.dram_tensor("x", [T, D], F32, kind="ExternalInput").ap()
    wqt = nc.dram_tensor("wqt", [128, 8 * 1024], F32, kind="ExternalInput").ap()
    wkt = nc.dram_tensor("wkt", [128, 8 * 1024], F32, kind="ExternalInput").ap()
    wvt = nc.dram_tensor("wvt", [128, 8 * 1024], F32, kind="ExternalInput").ap()
    wot = nc.dram_tensor("wot", [128, 8 * 1024], F32, kind="ExternalInput").ap()
    bo_in = nc.dram_tensor("bo_bc", [128, D], F32, kind="ExternalInput").ap()
    w1t = nc.dram_tensor("w1t", [32, 128, 8 * 128], F32, kind="ExternalInput").ap()
    w2t = nc.dram_tensor("w2t", [32, 128, 8 * 128], F32, kind="ExternalInput").ap()
    h1t = nc.dram_tensor("h1t", [32, 128, 32 * 128], F32, kind="ExternalInput").ap()
    h2t = nc.dram_tensor("h2t", [32, 128, 32 * 128], F32, kind="ExternalInput").ap()
    w3t = nc.dram_tensor("w3t", [8, 128, 32 * 128], F32, kind="ExternalInput").ap()
    s1_in = nc.dram_tensor("s1r", [128, 32], F32, kind="ExternalInput").ap()
    s2_in = nc.dram_tensor("s2r", [128, 32], F32, kind="ExternalInput").ap()
    sh1_in = nc.dram_tensor("sh1r", [128, 32], F32, kind="ExternalInput").ap()
    sh2_in = nc.dram_tensor("sh2r", [128, 32], F32, kind="ExternalInput").ap()
    s3_in = nc.dram_tensor("s3r", [128, 8], F32, kind="ExternalInput").ap()
    dist_in = nc.dram_tensor("dist", [128, 2 * 3968], F16, kind="ExternalInput").ap()
    idf_in = nc.dram_tensor("identf", [128, 128], F32, kind="ExternalInput").ap()
    sel_in = nc.dram_tensor("selrow", [128, 256], F32, kind="ExternalInput").ap()
    si_in = nc.dram_tensor("slopei", [128, 8 * 128], F16, kind="ExternalInput").ap()
    out_hbm = nc.dram_tensor("out", [T, D], F32, kind="ExternalOutput").ap()
    xn_dram = nc.dram_tensor("xn_scr", [8, 128, 8 * 512], BF16, kind="Internal").ap()
    attn_dram = nc.dram_tensor("attn_scr", [T, D], F32, kind="Internal").ap()

    esg = ExitStack()

    def sbg(name, shape, dtype):
        return esg.enter_context(nc.sbuf_tensor(name, shape, dtype))

    identf = sbg("idf_sb", [128, 128], F32)
    selrow = sbg("sel_sb", [128, 256], BF16)
    s1_sb = sbg("s1_sb", [128, 32], F32)
    s2_sb = sbg("s2_sb", [128, 32], F32)
    sh1_sb = sbg("sh1_sb", [128, 32], F32)
    sh2_sb = sbg("sh2_sb", [128, 32], F32)
    s3_sb = sbg("s3_sb", [128, 8], F32)
    biasC = sbg("biasC", [128, 1], F32)

    pp = [esg.enter_context(nc.psum_tensor("pp%d" % i, [128, 1024], F32))
          for i in range(4)]
    ps = [pp[i // 2][:, (i % 2) * 512:(i % 2) * 512 + 512] for i in range(8)]
    pse_pv = ps[4]
    pso_pv = ps[5]
    ps6_lo = pp[3][0:64, 0:512]
    ps7_hi = pp[3][64:128, 512:1024]
    psb = [Buf() for _ in range(8)]

    ld_idf = P.add("gp", lambda e: e.dma_start(identf[:, :], idf_in),
                   sem="dC", dma=True)
    ld_sel = P.add("gp", lambda e: e.dma_start(selrow[:, :], sel_in),
                   sem="dC", dma=True)
    scl = {}
    for nm, src, dst in (("s1", s1_in, s1_sb), ("s2", s2_in, s2_sb),
                         ("sh1", sh1_in, sh1_sb), ("sh2", sh2_in, sh2_sb),
                         ("s3", s3_in, s3_sb)):
        scl[nm] = P.add("sy", lambda e, d=dst, s=src: e.dma_start(d[:, :], s),
                        sem="dCs", dma=True)
    op_biasC = P.add("gp", lambda e: e.memset(biasC[:, :], EXP_SHIFT))

    xn_grp_writers = [[] for _ in range(8)]
    attn_row_writers = [[] for _ in range(32)]

    es_w = ExitStack()

    def sbw(name, shape, dtype):
        return es_w.enter_context(nc.sbuf_tensor(name, shape, dtype))

    wq_sb = sbw("wq_sb", [128, 8 * 1024], BF16)
    wo_sb = sbw("wo_sb", [128, 8 * 1024], BF16)
    wk_sb = sbw("wk_sb", [128, 8 * 1024], BF16)
    wv_sb = sbw("wv_sb", [128, 8 * 1024], BF16)
    K_dup = sbw("K_dup", [128, 8 * 2048], BF16)
    V_sb = sbw("V_sb", [128, 16 * 8 * 192], BF16)
    xn_at = sbw("xn_at", [128, 2 * 8 * 512], BF16)

    K_b = [[Buf() for _ in range(NKT)] for _ in range(8)]
    V_b = [[Buf() for _ in range(NKT)] for _ in range(8)]
    xn_at_b = [Buf(), Buf()]
    Q_b = [Buf() for _ in range(8)]
    P_b = [Buf() for _ in range(4)]
    PV_b = [Buf() for _ in range(8)]
    sums_b = [Buf() for _ in range(8)]
    bc_b = Buf()
    ast_b = Buf()
    dist_b = Buf()

    def wload(dst_t, src):
        return P.add("gp", lambda e, d=dst_t, s=src: e.dma_start(d[:, :], s),
                     sem="dC", dma=True)

    ld_wq = wload(wq_sb, wqt)
    ld_wo = wload(wo_sb, wot)
    ld_wk = wload(wk_sb, wkt)
    ld_wv = wload(wv_sb, wvt)

    op_vz = P.add("gp", lambda e: e.memset(V_sb[:, :], 0.0))
    ones_ap = bass.AP(V_sb[:, :].tensor, 64,
                      [[16 * 8 * 192, 128], [192, 16 * 8], [1, 1]])
    op_vo = P.add("gp", lambda e: e.memset(ones_ap, 1.0), deps=[op_vz])
    v_init = [op_vo]

    def xn_slot_load(slot, scol, extra):
        dst = xn_at[:, slot * 4096:(slot + 1) * 4096]
        g = scol // 512
        op = P.add("sy", lambda e, d=dst, s=xn_dram[g]: e.dma_start(d, s),
                   deps=xn_at_b[slot].wdeps() + xn_grp_writers[g] + extra,
                   sem="dX%d" % slot, dma=True)
        xn_at_b[slot].wrote(op)
        return op


    def kv_chunk(s, j):
        slot = j % 2
        o_x = xn_slot_load(slot, s * S + j * 512, [])
        xslot = xn_at[:, slot * 4096:(slot + 1) * 4096]
        for c in range(8):
            bank = c % 4
            mms = []
            for i in range(8):
                dep = (psb[bank].wdeps() + [ld_wk, o_x]) if i == 0 else []
                mm = P.add("te", lambda e, ii=i, cc=c, bb=bank, xs=xslot:
                           e.matmul(ps[bb],
                                    wk_sb[:, ii * 1024 + cc * 128:
                                          ii * 1024 + (cc + 1) * 128],
                                    xs[:, ii * 512:(ii + 1) * 512],
                                    start=(ii == 0), stop=(ii == 7)),
                           deps=dep)
                mms.append(mm)
            xn_at_b[slot].read(mms[-1])
            psb[bank].wrote(mms[-1])
            kt0 = j * 4
            kdeps = []
            for kk in range(4):
                kdeps += K_b[c][kt0 + kk].wdeps()
            dst = K_dup[:, c * 2048 + j * 512:c * 2048 + (j + 1) * 512]
            o_ev = P.add("vx", lambda e, d=dst, bb=bank: e.tensor_copy(
                d, ps[bb]), deps=[mms[-1]] + kdeps)
            psb[bank].read(o_ev)
            for kk in range(4):
                K_b[c][kt0 + kk].wrote(o_ev)
        for tt4 in range(4):
            kt = j * 4 + tt4
            for half in range(2):
                bank = (tt4 * 2 + half) % 4
                mms = []
                for i in range(8):
                    dep = (psb[bank].wdeps() + [ld_wv, o_x] + v_init) \
                        if i == 0 else []
                    mm = P.add("te", lambda e, ii=i, hh=half, t4=tt4,
                               bb=bank, xs=xslot:
                               e.matmul(ps[bb],
                                        xs[:, ii * 512 + t4 * 128:
                                           ii * 512 + (t4 + 1) * 128],
                                        wv_sb[:, ii * 1024 + hh * 512:
                                              ii * 1024 + (hh + 1) * 512],
                                        start=(ii == 0), stop=(ii == 7)),
                               deps=dep)
                    mms.append(mm)
                xn_at_b[slot].read(mms[-1])
                psb[bank].wrote(mms[-1])
                for h2 in range(4):
                    kvh = half * 4 + h2
                    blk = (kt * 8 + kvh) * 192
                    vb = V_b[kvh][kt]
                    dst = bass.AP(V_sb[:, :].tensor, blk,
                                  [[16 * 8 * 192, 128], [128, 2], [1, 64]])
                    o_ev = P.add("vx", lambda e, d=dst, bb=bank, hx=h2:
                                 e.tensor_copy(
                                     d, ps[bb][:, hx * 128:(hx + 1) * 128]),
                                 deps=[mms[-1]] + vb.wdeps() + v_init)
                    psb[bank].read(o_ev)
                    vb.wrote(o_ev)

    es1 = ExitStack()

    def sb1(name, shape, dtype):
        return es1.enter_context(nc.sbuf_tensor(name, shape, dtype))

    x_tile = sb1("x_tile", [128, 3 * D], F32)
    x_tile_b = [Buf() for _ in range(3)]
    xn_tm = sb1("xn_tm", [128, 2 * D], F32)
    xn_tm_b = [Buf(), Buf()]
    sq_scr = sb1("sq_scr", [128, 2 * D], BF16)
    sq_b = [Buf(), Buf()]
    stat = sb1("stat", [128, 24], F32)
    stat_b = [Buf() for _ in range(4)]
    xn_stage = sb1("xn_stage", [128, 2 * 4096], BF16)
    xn_stage_b = [[Buf() for _ in range(8)] for _ in range(2)]

    for tt in range(32):
        sl3, sl2, sl4 = tt % 3, tt % 2, tt % 4
        g = tt // 4
        gs = g % 2
        xa = x_tile[:, sl3 * D:(sl3 + 1) * D]
        xb = x_tile_b[sl3]
        ldx = P.add("sy", lambda e, d=xa, o=tt: e.dma_start(
            d, x_hbm[o * 128:(o + 1) * 128, :]),
            deps=xb.wdeps(), sem="dT%d" % sl3, dma=True)
        xb.wrote(ldx)
        st = stat[:, sl4 * 6:(sl4 + 1) * 6]
        sbu = stat_b[sl4]
        o_sum = P.add("vx", lambda e, s=xa, c=st: e.reduce_sum(
            c[:, 0:1], s, axis=AX.X, opt_input=False, opt_output=False),
            deps=xb.rdeps() + sbu.wdeps())
        xb.read(o_sum)
        sbu.wrote(o_sum)
        o_sq = P.add("sc", lambda e, s=xa, c=st,
                     q=sq_scr[:, sl2 * D:(sl2 + 1) * D]: e.activation(
                         q, s, AF.Square, accum_out=c[:, 1:2]),
                     deps=xb.rdeps() + [o_sum] + sq_b[sl2].wdeps())
        xb.read(o_sq)
        sq_b[sl2].wrote(o_sq)
        sbu.also_wrote(o_sq)
        o_mu = P.add("vx", lambda e, c=st: e.tensor_scalar_mul(
            c[:, 2:3], c[:, 0:1], 1.0 / D), deps=[o_sum, o_sq])
        o_ex2 = P.add("vx", lambda e, c=st: e.tensor_scalar_mul(
            c[:, 3:4], c[:, 1:2], 1.0 / D), deps=[o_sq])
        o_nmu2 = P.add("vx", lambda e, c=st: e.scalar_tensor_tensor(
            c[:, 4:5], c[:, 2:3], -1.0, c[:, 2:3], op0=ALU.mult, op1=ALU.mult),
            deps=[o_mu])
        o_var = P.add("vx", lambda e, c=st: e.tensor_add(
            c[:, 3:4], c[:, 3:4], c[:, 4:5]), deps=[o_ex2, o_nmu2])
        o_veps = P.add("vx", lambda e, c=st: e.tensor_scalar_add(
            c[:, 3:4], c[:, 3:4], EPS), deps=[o_var])
        o_std = P.add("sc", lambda e, c=st: e.activation(
            c[:, 4:5], c[:, 3:4], AF.Sqrt), deps=[o_veps])
        o_rstd = P.add("vx", lambda e, c=st: e.reciprocal(
            c[:, 5:6], c[:, 4:5]), deps=[o_std])
        for o in (o_mu, o_ex2, o_nmu2, o_var, o_veps, o_std, o_rstd):
            sbu.also_wrote(o)
        o_gate = P.add("sc", lambda e: e.nop(), deps=[o_rstd, o_mu])
        xt = xn_tm[:, sl2 * D:(sl2 + 1) * D]
        o_norm = P.add("vx", lambda e, s=xa, d=xt, c=st: e.tensor_scalar(
            d, s, c[:, 2:3], c[:, 5:6], op0=ALU.subtract, op1=ALU.mult),
            deps=xb.rdeps() + [o_gate] + xn_tm_b[sl2].wdeps())
        xb.read(o_norm)
        sbu.read(o_norm)
        xn_tm_b[sl2].wrote(o_norm)
        for dch in range(8):
            bank = 6 + (dch % 2)
            o_tr = P.add("te", lambda e, s=xt[:, dch * 128:(dch + 1) * 128],
                         bb=bank: e.transpose(ps[bb][:, 0:128], s, identf[:, :]),
                         deps=[ld_idf, o_norm] + psb[bank].wdeps())
            xn_tm_b[sl2].read(o_tr)
            psb[bank].wrote(o_tr)
            stb = xn_stage_b[gs][dch]
            dst = xn_stage[:, gs * 4096 + dch * 512 + sl4 * 128:
                           gs * 4096 + dch * 512 + (sl4 + 1) * 128]
            o_ev = P.add("vx", lambda e, d=dst, bb=bank: e.tensor_copy(
                d, ps[bb][:, 0:128]),
                deps=[o_tr] + (stb.wdeps() if sl4 == 0 else []))
            psb[bank].read(o_ev)
            if sl4 == 0:
                stb.wrote(o_ev)
            else:
                stb.also_wrote(o_ev)
        if sl4 == 3:
            for dch in range(8):
                stb = xn_stage_b[gs][dch]
                src = xn_stage[:, gs * 4096 + dch * 512:gs * 4096 + (dch + 1) * 512]
                o_st = P.add("sy", lambda e, s=src, dd=dch, gg=g: e.dma_start(
                    xn_dram[gg][:, dd * 512:(dd + 1) * 512], s),
                    deps=stb.wdeps(), sem="dXS", dma=True)
                stb.read(o_st)
                xn_grp_writers[g].append(o_st)
            if g < 4:
                kv_chunk(0, g)

    ln_fence = P.fence()
    es1.close()

    es2 = ExitStack()

    def sb2(name, shape, dtype):
        return es2.enter_context(nc.sbuf_tensor(name, shape, dtype))

    dist_w = sb2("dist_w", [128, 2 * DW], F16)
    si_sb = sb2("si_sb", [128, 8 * 128], F16)
    bo_sb = sb2("bo_sb", [128, D], F32)
    ld_bo = P.add("sy", lambda e: e.dma_start(bo_sb[:, :], bo_in),
                  deps=ln_fence, sem="dCs", dma=True)
    ld_si = P.add("sy", lambda e: e.dma_start(si_sb[:, :], si_in),
                  deps=ln_fence, sem="dCs", dma=True)
    Q_sb = sb2("Q_sb", [128, 8 * 512], BF16)
    P_sb = sb2("P_sb", [128, 4 * 512], BF16)
    PV_sb = sb2("PV_sb", [128, 8 * 512], BF16)
    sums_sb = sb2("sums_sb", [128, 2 * 512], BF16)
    bc_sb = sb2("bc_sb", [128, 512], F32)
    ast = sb2("ast", [128, D], F32)

    for s in range(2):
        if s == 1:
            for j in range(4):
                kv_chunk(1, j)
        for qc in range(NQC):
            xslot = (s * NQC + qc) % 2
            o_x = xn_slot_load(xslot, s * S + qc * 512, [])
            xs = xn_at[:, xslot * 4096:(xslot + 1) * 4096]
            dsrc = bass.AP(dist_in.tensor, qc * 512,
                           [[2 * 3968, 128], [3968, 2], [1, DW]])
            o_dist = P.add("sy", lambda e, s2=dsrc: e.dma_start(
                dist_w[:, :].rearrange("p (s c) -> p s c", s=2), s2),
                deps=dist_b.wdeps() + (ln_fence if s == 0 and qc == 0 else []),
                sem="dDW", dma=True)
            dist_b.wrote(o_dist)
            for hp in range(8):
                bank = hp % 2
                mms = []
                for i in range(8):
                    dep = (psb[bank].wdeps() + [ld_wq, o_x]) if i == 0 else []
                    mm = P.add("te", lambda e, ii=i, h=hp, bb=bank, x2=xs:
                               e.matmul(ps[bb],
                                        wq_sb[:, ii * 1024 + h * 128:
                                              ii * 1024 + (h + 1) * 128],
                                        x2[:, ii * 512:(ii + 1) * 512],
                                        start=(ii == 0), stop=(ii == 7)),
                               deps=dep)
                    mms.append(mm)
                xn_at_b[xslot].read(mms[-1])
                psb[bank].wrote(mms[-1])
                o_ev = P.add("vx", lambda e, h=hp, bb=bank: e.tensor_copy(
                    Q_sb[:, h * 512:(h + 1) * 512], ps[bb]),
                    deps=[mms[-1]] + Q_b[hp].wdeps()
                    + (ln_fence if (s == 0 and qc == 0) else []))
                psb[bank].read(o_ev)
                Q_b[hp].wrote(o_ev)

            for hp in range(8):
                kvh = hp
                slope_e = float(2.0 ** (-(2 * hp + 1)))
                slope_o = float(2.0 ** (-(2 * hp + 2)))
                pv_e_mm = pv_o_mm = None
                exps = {}

                def emit_scores(kt):
                    bp = kt % 2
                    bank_e, bank_o = bp * 2, bp * 2 + 1
                    uloc = 1920 - 128 * kt
                    kslc = K_dup[:, kvh * 2048 + kt * 128:
                                 kvh * 2048 + (kt + 1) * 128]
                    mm_se = P.add("te", lambda e, ks=kslc, h=hp, bb=bank_e:
                                  e.matmul(ps[bb], ks[0:64, :],
                                           Q_sb[0:64, h * 512:(h + 1) * 512],
                                           start=True, stop=False,
                                           tile_position=(0, 0)),
                                  deps=K_b[kvh][kt].rdeps() + Q_b[hp].rdeps()
                                  + psb[bank_e].wdeps())
                    K_b[kvh][kt].read(mm_se)
                    Q_b[hp].read(mm_se)
                    psb[bank_e].wrote(mm_se)
                    mm_so = P.add("te", lambda e, ks=kslc, h=hp, bb=bank_o:
                                  e.matmul(ps[bb], ks[64:128, :],
                                           Q_sb[64:128, h * 512:(h + 1) * 512],
                                           start=True, stop=False,
                                           tile_position=(64, 0)),
                                  deps=K_b[kvh][kt].rdeps() + Q_b[hp].rdeps()
                                  + psb[bank_o].wdeps())
                    K_b[kvh][kt].read(mm_so)
                    Q_b[hp].read(mm_so)
                    psb[bank_o].wrote(mm_so)
                    mm_be = P.add("te", lambda e, h=hp, u=uloc, bb=bank_e:
                                  e.matmul(ps[bb], si_sb[:, h * 128:(h + 1) * 128],
                                           dist_w[:, u:u + 512],
                                           start=False, stop=True),
                                  deps=[o_dist, ld_si])
                    psb[bank_e].also_wrote(mm_be)
                    dist_b.read(mm_be)
                    mm_bo = P.add("te", lambda e, h=hp, u=uloc, bb=bank_o:
                                  e.matmul(ps[bb], si_sb[:, h * 128:(h + 1) * 128],
                                           dist_w[:, DW + u:DW + u + 512],
                                           start=False, stop=True),
                                  deps=[o_dist, ld_si])
                    psb[bank_o].also_wrote(mm_bo)
                    dist_b.read(mm_bo)
                    ppair = P_sb[:, bp * 1024:(bp + 1) * 1024]
                    expp = P.add("sc", lambda e, d=ppair, b2=bp: e.activation(
                        d, pp[b2][:, :], AF.Exp, bias=biasC[:, :], scale=0.125),
                        deps=[mm_be, mm_bo, op_biasC] + P_b[bp].wdeps())
                    psb[bank_e].read(expp)
                    psb[bank_o].read(expp)
                    P_b[bp].wrote(expp)
                    exps[kt] = (expp, P_sb[:, bp * 1024:bp * 1024 + 512],
                                P_sb[:, bp * 1024 + 512:(bp + 1) * 1024])

                def emit_pv(kt):
                    nonlocal pv_e_mm, pv_o_mm
                    bp = kt % 2
                    expp, pe, po = exps.pop(kt)
                    exp_e = exp_o = expp
                    blk = (kt * 8 + kvh) * 192
                    depe = (psb[4].wdeps() if kt == 0 else []) + [exp_e] \
                        + V_b[kvh][kt].rdeps()
                    pv_e_mm = P.add("te", lambda e, b2=blk, p2=pe, k2=kt:
                                    e.matmul(pse_pv,
                                             V_sb[:, b2:b2 + 128], p2,
                                             start=(k2 == 0), stop=(k2 == NKT - 1)),
                                    deps=depe)
                    V_b[kvh][kt].read(pv_e_mm)
                    P_b[bp].read(pv_e_mm)
                    psb[4].wrote(pv_e_mm) if kt == 0 else psb[4].also_wrote(pv_e_mm)
                    depo = (psb[5].wdeps() if kt == 0 else []) + [exp_o] \
                        + V_b[kvh][kt].rdeps()
                    pv_o_mm = P.add("te", lambda e, b2=blk, p2=po, k2=kt:
                                    e.matmul(pso_pv,
                                             V_sb[:, b2 + 64:b2 + 192], p2,
                                             start=(k2 == 0), stop=(k2 == NKT - 1)),
                                    deps=depo)
                    V_b[kvh][kt].read(pv_o_mm)
                    P_b[bp].read(pv_o_mm)
                    psb[5].wrote(pv_o_mm) if kt == 0 else psb[5].also_wrote(pv_o_mm)

                for kt in range(NKT):
                    emit_scores(kt)
                    if kt >= 1:
                        emit_pv(kt - 1)
                emit_pv(NKT - 1)
                pvb = PV_b[hp]
                smb = sums_b[hp % 2]
                c1 = P.add("vx", lambda e, h=hp: e.tensor_copy(
                    PV_sb[0:64, h * 512:(h + 1) * 512], pse_pv[0:64, :]),
                    deps=[pv_e_mm] + pvb.wdeps())
                psb[4].read(c1)
                c2 = P.add("vx", lambda e, h=hp: e.tensor_copy(
                    sums_sb[64:65, (h % 2) * 512:(h % 2 + 1) * 512], pse_pv[64:65, :]),
                    deps=[pv_e_mm] + smb.wdeps())
                psb[4].read(c2)
                c3 = P.add("vx", lambda e, h=hp: e.tensor_copy(
                    PV_sb[64:128, h * 512:(h + 1) * 512], pso_pv[64:128, :]),
                    deps=[pv_o_mm] + pvb.wdeps())
                psb[5].read(c3)
                c4 = P.add("vx", lambda e, h=hp: e.tensor_copy(
                    sums_sb[0:1, (h % 2) * 512:(h % 2 + 1) * 512], pso_pv[0:1, :]),
                    deps=[pv_o_mm] + smb.wdeps())
                psb[5].read(c4)
                pvb.wrote(c1)
                pvb.also_wrote(c3)
                smb.wrote(c2)
                smb.also_wrote(c4)
                m1 = P.add("te", lambda e, h=hp: e.matmul(
                    ps[6], selrow[64:65, 0:128],
                    sums_sb[64:65, (h % 2) * 512:(h % 2 + 1) * 512],
                    start=True, stop=True),
                    deps=[c2, ld_sel] + psb[6].wdeps())
                psb[6].wrote(m1)
                smb.read(m1)
                m2 = P.add("te", lambda e, h=hp: e.matmul(
                    ps[7], selrow[0:1, 128:256],
                    sums_sb[0:1, (h % 2) * 512:(h % 2 + 1) * 512],
                    start=True, stop=True),
                    deps=[c4, ld_sel] + psb[7].wdeps())
                psb[7].wrote(m2)
                smb.read(m2)
                r1 = P.add("vx", lambda e: e.reciprocal(
                    bc_sb[0:64, :], ps6_lo), deps=[m1] + bc_b.wdeps())
                psb[6].read(r1)
                r2 = P.add("vx", lambda e: e.reciprocal(
                    bc_sb[64:128, :], ps7_hi), deps=[m2] + bc_b.wdeps())
                psb[7].read(r2)
                bc_b.wrote(r1)
                bc_b.also_wrote(r2)
                n1 = P.add("vx", lambda e, h=hp: e.tensor_mul(
                    PV_sb[0:64, h * 512:(h + 1) * 512],
                    PV_sb[0:64, h * 512:(h + 1) * 512], bc_sb[0:64, :]),
                    deps=[r1, c1])
                bc_b.read(n1)
                pvb.also_wrote(n1)
                n2 = P.add("vx", lambda e, h=hp: e.tensor_mul(
                    PV_sb[64:128, h * 512:(h + 1) * 512],
                    PV_sb[64:128, h * 512:(h + 1) * 512], bc_sb[64:128, :]),
                    deps=[r2, c3])
                bc_b.read(n2)
                pvb.also_wrote(n2)

            pv_all_writers = [w for b in PV_b for w in b.writers]
            for j in range(4):
                bank = j % 2
                tok0 = s * S + qc * 512 + j * 128
                evs = []
                for half in range(2):
                    mms = []
                    for h in range(8):
                        dep = []
                        if h == 0:
                            dep = psb[bank].wdeps() + [ld_wo] + pv_all_writers
                        mm = P.add("te", lambda e, h2=h, j2=j, hf=half, bb=bank:
                                   e.matmul(ps[bb],
                                            PV_sb[:, h2 * 512 + j2 * 128:
                                                  h2 * 512 + (j2 + 1) * 128],
                                            wo_sb[:, h2 * 1024 + hf * 512:
                                                  h2 * 1024 + (hf + 1) * 512],
                                            start=(h2 == 0), stop=(h2 == 7)),
                                   deps=dep)
                        mms.append(mm)
                    for b in PV_b:
                        b.read(mms[-1])
                    psb[bank].wrote(mms[-1])
                    dst = ast[:, half * 512:(half + 1) * 512]
                    o_ev = P.add("vx", lambda e, d=dst, hf=half, bb=bank:
                                 e.tensor_add(d, ps[bb],
                                              bo_sb[:, hf * 512:(hf + 1) * 512]),
                                 deps=[mms[-1], ld_bo]
                                 + (ast_b.wdeps() if half == 0 else []))
                    psb[bank].read(o_ev)
                    if half == 0:
                        ast_b.wrote(o_ev)
                    else:
                        ast_b.also_wrote(o_ev)
                    evs.append(o_ev)
                o_st = P.add("sy", lambda e, t0=tok0: e.dma_start(
                    attn_dram[t0:t0 + 128, :], ast[:, :]),
                    deps=evs, sem="dA", dma=True)
                ast_b.read(o_st)
                attn_row_writers[tok0 // 128].append(o_st)

    attn_fence = P.fence()
    es2.close()
    es_w.close()

    es3 = ExitStack()

    def sb3(name, shape, dtype):
        return es3.enter_context(nc.sbuf_tensor(name, shape, dtype))

    H_sb = sb3("H_sb", [128, 32 * 1024], BF16)
    G_sb = sb3("G_sb", [128, 32 * 1024], BF16)
    wst = sb3("wst", [128, 2 * 4096], BF16)
    xn_fc = sb3("xn_fc", [128, 8 * 1024], BF16)
    ffnE = sb3("ffnE", [128, 2 * 512], F32)
    ost = sb3("ost", [128, 4 * 1024], F32)
    atr = sb3("atr", [128, 2 * 1024], F32)
    xrl = sb3("xrl", [128, 2 * 1024], F32)

    H_b = [Buf() for _ in range(32)]
    G_b = [Buf() for _ in range(32)]
    wst_b = [Buf(), Buf()]
    xn_fc_b = Buf()
    ffnE_b = [Buf(), Buf()]
    ost_b = [Buf() for _ in range(4)]
    atr_b = [Buf(), Buf()]
    xrl_b = [Buf(), Buf()]

    wst_ctr = [0]

    def next_wslot():
        s = wst_ctr[0] % 2
        wst_ctr[0] += 1
        return s

    def wstream(slot, src_slice, extra, nrows):
        dst = wst[:, slot * 4096:slot * 4096 + nrows * 128]
        if src_slice.ndim == 3:
            dst = dst.rearrange("p (i o) -> p i o", i=src_slice.shape[1])
        op = P.add("gp", lambda e, d=dst, s=src_slice: e.dma_start(d, s),
                   deps=wst_b[slot].wdeps() + extra, sem="dW%d" % slot, dma=True)
        wst_b[slot].wrote(op)
        return op

    def emit_xn_load(fc, fence0):
        ops = []
        for g2 in range(2):
            g = 2 * fc + g2
            op = P.add("sc", lambda e, gg=g, g3=g2: e.dma_start(
                xn_fc[:, g3 * 4096:(g3 + 1) * 4096], xn_dram[gg]),
                deps=(xn_fc_b.wdeps() if g2 == 0 else []) + xn_grp_writers[g]
                + fence0, sem="dXF", dma=True)
            ops.append(op)
        xn_fc_b.wrote(ops[0])
        xn_fc_b.also_wrote(ops[1])
        return ops

    for fc in range(NFC):
        if fc == 0:
            o_xns = emit_xn_load(0, attn_fence)
        else:
            o_xns = nxt
        for which in ("w1", "w2"):
            wt = w1t if which == "w1" else w2t
            for op2 in range(16):
                slot = next_wslot()
                wsrc = wt[2 * op2:2 * op2 + 2].transpose([1, 0, 2])
                extra = attn_fence if (fc == 0 and which == "w1" and op2 < 2) else []
                o_w = wstream(slot, wsrc, extra, 16)
                for oh in range(2):
                    o = 2 * op2 + oh
                    bankp = (o % 2) * 2
                    mm_last = [None, None]
                    for i in range(8):
                        for tc2 in range(2):
                            bank = bankp + tc2
                            dep = (psb[bank].wdeps() + [o_w] + o_xns) if i == 0 else []
                            mm = P.add("te", lambda e, ii=i, t2=tc2, bb=bankp,
                                       sl=slot, o2=oh:
                                       e.matmul(ps[bb + t2],
                                                wst[:, sl * 4096 + o2 * 1024
                                                    + ii * 128:
                                                    sl * 4096 + o2 * 1024
                                                    + (ii + 1) * 128],
                                                xn_fc[:, t2 * 4096 + ii * 512:
                                                      t2 * 4096 + ii * 512 + 512],
                                                start=(ii == 0), stop=(ii == 7)),
                                       deps=dep)
                            if i == 0:
                                psb[bank].wrote(mm)
                            elif i == 7:
                                psb[bank].also_wrote(mm)
                            mm_last[tc2] = mm
                    for tc2 in range(2):
                        wst_b[slot].read(mm_last[tc2])
                        xn_fc_b.read(mm_last[tc2])
                    for tc2 in range(2):
                        bank = bankp + tc2
                        hcol = o * 1024 + tc2 * 512
                        if which == "w1":
                            o_ev = P.add("sc", lambda e, hc=hcol, oo=o, bb=bank:
                                         e.activation(H_sb[:, hc:hc + 512],
                                                      ps[bb], AF.Silu,
                                                      scale=s1_sb[:, oo:oo + 1]),
                                         deps=[mm_last[tc2], scl["s1"]]
                                         + H_b[o].wdeps())
                        else:
                            o_ev = P.add("vx", lambda e, hc=hcol, oo=o, bb=bank:
                                         e.scalar_tensor_tensor(
                                             H_sb[:, hc:hc + 512], ps[bb],
                                             s2_sb[:, oo:oo + 1],
                                             H_sb[:, hc:hc + 512],
                                             op0=ALU.mult, op1=ALU.mult),
                                         deps=[mm_last[tc2], scl["s2"]]
                                         + (H_b[o].rdeps() if which == "w2" else []))
                        psb[bank].read(o_ev)
                        if which == "w1" and tc2 == 0:
                            H_b[o].wrote(o_ev)
                        else:
                            H_b[o].also_wrote(o_ev)
        if fc + 1 < NFC:
            nxt = emit_xn_load(fc + 1, [])
        for which in ("h1", "h2"):
            ht = h1t if which == "h1" else h2t
            ssb = sh1_sb if which == "h1" else sh2_sb
            sscl = scl["sh1"] if which == "h1" else scl["sh2"]
            SRC_b, DST_b = (H_b, G_b) if which == "h1" else (G_b, H_b)
            src_t, dst_t = (H_sb, G_sb) if which == "h1" else (G_sb, H_sb)
            src_writers = [w for b in SRC_b for w in b.writers]
            for o in range(32):
                slot = next_wslot()
                wsrc = ht[o]
                o_w = wstream(slot, wsrc, [], 32)
                bankp = (o % 2) * 2
                mm_last = [None, None]
                for i in range(32):
                    for tc2 in range(2):
                        bank = bankp + tc2
                        dep = []
                        if i == 0:
                            dep = psb[bank].wdeps() + [o_w] + src_writers
                        mm = P.add("te", lambda e, ii=i, t2=tc2, bb=bankp,
                                   sl=slot, st2=src_t:
                                   e.matmul(ps[bb + t2],
                                            wst[:, sl * 4096 + ii * 128:
                                                sl * 4096 + (ii + 1) * 128],
                                            st2[:, ii * 1024 + t2 * 512:
                                                ii * 1024 + t2 * 512 + 512],
                                            start=(ii == 0), stop=(ii == 31)),
                                   deps=dep)
                        if i == 0:
                            psb[bank].wrote(mm)
                        elif i == 31:
                            psb[bank].also_wrote(mm)
                        mm_last[tc2] = mm
                for tc2 in range(2):
                    wst_b[slot].read(mm_last[tc2])
                    for b in SRC_b:
                        b.read(mm_last[tc2])
                for tc2 in range(2):
                    bank = bankp + tc2
                    hcol = o * 1024 + tc2 * 512
                    o_ev = P.add("sc", lambda e, hc=hcol, oo=o, bb=bank,
                                 d2=dst_t, s3=ssb:
                                 e.activation(d2[:, hc:hc + 512], ps[bb],
                                              AF.Gelu, scale=s3[:, oo:oo + 1]),
                                 deps=[mm_last[tc2], sscl] + DST_b[o].wdeps())
                    psb[bank].read(o_ev)
                    if tc2 == 0:
                        DST_b[o].wrote(o_ev)
                    else:
                        DST_b[o].also_wrote(o_ev)

        h_writers = [w for b in H_b for w in b.writers]

        def emit_reload(fc2, tc2, j2, fen):
            tok0 = fc2 * 1024 + tc2 * 512 + j2 * 128
            rslot = j2 % 2
            o_ar = P.add("sc", lambda e, t0=tok0, rs=rslot: e.dma_start(
                atr[:, rs * 1024:(rs + 1) * 1024], attn_dram[t0:t0 + 128, :]),
                deps=atr_b[rslot].wdeps() + attn_row_writers[tok0 // 128] + fen,
                sem="dAR%d" % rslot, dma=True)
            atr_b[rslot].wrote(o_ar)
            o_xr = P.add("sc", lambda e, t0=tok0, rs=rslot: e.dma_start(
                xrl[:, rs * 1024:(rs + 1) * 1024], x_hbm[t0:t0 + 128, :]),
                deps=xrl_b[rslot].wdeps() + fen, sem="dXR%d" % rslot, dma=True)
            xrl_b[rslot].wrote(o_xr)
            return (o_ar, o_xr)

        for tc2 in range(2):
            reloads = {}
            fen0 = attn_fence if (fc == 0 and tc2 == 0) else []
            reloads[0] = emit_reload(fc, tc2, 0, fen0)
            reloads[1] = emit_reload(fc, tc2, 1, fen0)
            for m in range(8):
                slot = next_wslot()
                wsrc = w3t[m]
                o_w = wstream(slot, wsrc, [], 32)
                bank = m % 2
                mms = []
                for i in range(32):
                    dep = []
                    if i == 0:
                        dep = psb[bank].wdeps() + [o_w] + h_writers
                    mm = P.add("te", lambda e, ii=i, t2=tc2, bb=bank, sl=slot:
                               e.matmul(ps[bb],
                                        wst[:, sl * 4096 + ii * 128:
                                            sl * 4096 + (ii + 1) * 128],
                                        H_sb[:, ii * 1024 + t2 * 512:
                                             ii * 1024 + t2 * 512 + 512],
                                        start=(ii == 0), stop=(ii == 31)),
                               deps=dep)
                    mms.append(mm)
                wst_b[slot].read(mms[-1])
                for b in H_b:
                    b.read(mms[-1])
                psb[bank].wrote(mms[-1])
                eslot = m % 2
                o_ev = P.add("sc", lambda e, m2=m, bb=bank, es2=eslot:
                             e.activation(ffnE[:, es2 * 512:(es2 + 1) * 512],
                                          ps[bb], AF.Copy,
                                          scale=s3_sb[:, m2:m2 + 1]),
                             deps=[mms[-1], scl["s3"]] + ffnE_b[eslot].wdeps())
                psb[bank].read(o_ev)
                ffnE_b[eslot].wrote(o_ev)
                for j2 in range(4):
                    tbank = 4 + (j2 % 2)
                    o_tr = P.add("te", lambda e, es2=eslot, jj=j2, tb=tbank:
                                 e.transpose(ps[tb][:, 0:128],
                                             ffnE[:, es2 * 512 + jj * 128:
                                                  es2 * 512 + (jj + 1) * 128],
                                             identf[:, :]),
                                 deps=[o_ev, ld_idf] + psb[tbank].wdeps())
                    ffnE_b[eslot].read(o_tr)
                    psb[tbank].wrote(o_tr)
                    o_cp = P.add("vx", lambda e, jj=j2, m2=m, tb=tbank:
                                 e.tensor_copy(ost[:, jj * 1024 + m2 * 128:
                                                   jj * 1024 + (m2 + 1) * 128],
                                               ps[tb][:, 0:128]),
                                 deps=[o_tr]
                                 + (ost_b[j2].wdeps() if m == 0 else []))
                    psb[tbank].read(o_cp)
                    if m == 0:
                        ost_b[j2].wrote(o_cp)
                    else:
                        ost_b[j2].also_wrote(o_cp)
            for j2 in range(4):
                tok0 = fc * 1024 + tc2 * 512 + j2 * 128
                rslot = j2 % 2
                o_ar, o_xr = reloads[j2]
                o_a1 = P.add("vx", lambda e, jj=j2, rs=rslot: e.tensor_add(
                    ost[:, jj * 1024:(jj + 1) * 1024],
                    ost[:, jj * 1024:(jj + 1) * 1024],
                    atr[:, rs * 1024:(rs + 1) * 1024]),
                    deps=[o_ar] + ost_b[j2].wdeps())
                atr_b[rslot].read(o_a1)
                ost_b[j2].also_wrote(o_a1)
                o_a2 = P.add("vx", lambda e, jj=j2, rs=rslot: e.tensor_add(
                    ost[:, jj * 1024:(jj + 1) * 1024],
                    ost[:, jj * 1024:(jj + 1) * 1024],
                    xrl[:, rs * 1024:(rs + 1) * 1024]),
                    deps=[o_xr, o_a1])
                xrl_b[rslot].read(o_a2)
                ost_b[j2].also_wrote(o_a2)
                if j2 + 2 < 4:
                    reloads[j2 + 2] = emit_reload(fc, tc2, j2 + 2, [])
                o_st = P.add("sy", lambda e, t0=tok0, jj=j2: e.dma_start(
                    out_hbm[t0:t0 + 128, :], ost[:, jj * 1024:(jj + 1) * 1024]),
                    deps=[o_a2], sem="dO", dma=True)
                ost_b[j2].read(o_st)

    last_o = [op for op in P.ops["sy"] if op.sem == "dO"][-1]
    P.add("sy", lambda e: e.nop(), deps=[last_o])

    P.finalize_and_emit()
    es3.close()
    esg.close()
    return nc


def _prep_inputs(ins):
    f = np.float32
    Wq, Wk, Wv, Wo = (np.asarray(ins["L0_attn_Wq"], f), np.asarray(ins["L0_attn_Wk"], f),
                      np.asarray(ins["L0_attn_Wv"], f), np.asarray(ins["L0_attn_Wo"], f))
    def stream128(wT):
        nI = wT.shape[0] // 128
        return np.ascontiguousarray(
            wT.reshape(nI, 128, wT.shape[1]).transpose(1, 0, 2).reshape(
                128, nI * wT.shape[1]))

    def streamO(wT, ncols):
        nI = wT.shape[0] // 128
        nO = wT.shape[1] // ncols
        a = wT.reshape(nI, 128, nO, ncols).transpose(2, 1, 0, 3)
        return np.ascontiguousarray(a.reshape(nO, 128, nI * ncols))

    d = {}
    d["wqt"] = stream128(np.ascontiguousarray(Wq.T))
    kt = np.empty((D, D), f)
    for c in range(8):
        blk = Wk.T[:, c * 64:(c + 1) * 64]
        kt[:, c * 128:c * 128 + 64] = blk
        kt[:, c * 128 + 64:(c + 1) * 128] = blk
    d["wkt"] = stream128(kt)
    vt = np.empty((D, D), f)
    for h in range(8):
        blk = Wv.T[:, h * 64:(h + 1) * 64]
        vt[:, h * 128:h * 128 + 64] = blk
        vt[:, h * 128 + 64:(h + 1) * 128] = blk
    d["wvt"] = stream128(vt)
    d["wot"] = stream128(np.ascontiguousarray(Wo.T))
    d["bo_bc"] = np.ascontiguousarray(
        np.broadcast_to(np.asarray(ins["L0_attn_bo"], f)[None, :], (128, D)))
    d["w1t"] = streamO(np.asarray(ins["L0_ffn_w1_w"], f).T, 128)
    d["w2t"] = streamO(np.asarray(ins["L0_ffn_w2_w"], f).T, 128)
    d["h1t"] = streamO(np.asarray(ins["L0_ffn_h1_w"], f).T, 128)
    d["h2t"] = streamO(np.asarray(ins["L0_ffn_h2_w"], f).T, 128)
    d["w3t"] = streamO(np.asarray(ins["L0_ffn_w3_w"], f).T, 128)
    d["s1r"] = np.ascontiguousarray(np.asarray(ins["L0_ffn_w1_s"], f).reshape(32, 128).T)
    d["s2r"] = np.ascontiguousarray(np.asarray(ins["L0_ffn_w2_s"], f).reshape(32, 128).T)
    d["sh1r"] = np.ascontiguousarray(np.asarray(ins["L0_ffn_h1_s"], f).reshape(32, 128).T)
    d["sh2r"] = np.ascontiguousarray(np.asarray(ins["L0_ffn_h2_s"], f).reshape(32, 128).T)
    d["s3r"] = np.ascontiguousarray(np.asarray(ins["L0_ffn_w3_s"], f).reshape(8, 128).T)
    p = np.arange(128, dtype=f)[:, None]
    u = np.arange(3968, dtype=f)[None, :]
    dst0 = np.abs(p + 1920.0 - u)
    d["dist"] = np.concatenate([dst0 / 64.0, dst0 / 128.0], axis=1).astype(np.float16)
    si = np.zeros((128, 8 * 128), np.float16)
    for hp2 in range(8):
        np.fill_diagonal(si[:, hp2 * 128:(hp2 + 1) * 128], -(2.0 ** (5 - 2 * hp2)))
    d["slopei"] = si
    d["identf"] = np.eye(128, dtype=f)
    sel = np.zeros((128, 256), f)
    sel[64, 0:64] = 1.0
    sel[0, 128 + 64:256] = 1.0
    d["selrow"] = sel
    return d


def kernel(**inputs):
    x = np.asarray(inputs["L0_attn_x"], dtype=np.float32)
    shared = _prep_inputs(inputs)
    nc = build_program()
    in_maps = []
    for c in range(N_CORES):
        m = dict(shared)
        m["x"] = np.ascontiguousarray(x[2 * c:2 * c + 2].reshape(T, D))
        in_maps.append(m)
    res = run_bass_kernel_spmd(nc, in_maps, core_ids=list(range(N_CORES)))
    outs = [res.results[c]["out"].reshape(2, S, D) for c in range(N_CORES)]
    return np.ascontiguousarray(np.concatenate(outs, axis=0))


